# version 14
# baseline (speedup 1.0000x reference)
import os
import numpy as np
import ml_dtypes
import concourse.bass as bass
import concourse.mybir as mybir
from concourse.bass_utils import run_bass_kernel_spmd

F32 = mybir.dt.float32
BF16 = mybir.dt.bfloat16
AF = mybir.ActivationFunctionType
ALU = mybir.AluOpType
AX = mybir.AxisListType

S = 4096
C = 1024
NQ, NKV, D = 16, 4, 64
NCH = S // 512
NT = S // 128
ROPE_BASE = 10000.0
EPS = 1e-5
BF = ml_dtypes.bfloat16

_UPTO = os.environ.get("K_UPTO", "C")
_ORDER = ["A", "A2", "B", "W", "C"]
_NCHA = int(os.environ.get("K_ACH", "8"))
_NCHC = int(os.environ.get("K_CCH", "8"))


def _host_consts():
    ident = np.eye(128, dtype=BF)
    tri = np.triu(np.ones((128, 128), np.float32)).astype(BF)
    onescol = np.ones((128, 128), dtype=BF)
    pat2 = np.zeros((2, 128), np.float32)
    pat2[0, 0:64] = 1.0
    pat2[1, 64:128] = 1.0
    ones1f = np.ones((1, 128), np.float32)
    epst = np.full((128, 1), EPS, np.float32)
    inv = 1.0 / ROPE_BASE ** (np.arange(0, D, 2, dtype=np.float64) / D)
    ang = np.arange(S, dtype=np.float64)[:, None] * inv[None, :]
    cos64 = np.concatenate([np.cos(ang), np.cos(ang)], axis=1).T
    sin64 = np.concatenate([np.sin(ang), np.sin(ang)], axis=1).T
    sinp = sin64.copy()
    sinp[0:32, :] *= -1.0
    cosrep = np.tile(cos64, (2, 1)).astype(np.float32).astype(BF)
    sinrep = np.tile(sinp, (2, 1)).astype(np.float32).astype(BF)
    permQ = np.zeros((128, 128), np.float32)
    for i in range(32):
        permQ[32 + i, i] = 1.0
        permQ[i, 32 + i] = 1.0
        permQ[96 + i, 64 + i] = 1.0
        permQ[64 + i, 96 + i] = 1.0
    return dict(ident=ident, tri=tri, onescol=onescol, pat2=pat2,
                pat2T=np.ascontiguousarray(pat2.astype(BF).T),
                ones1f=ones1f, epst=epst, cosrep=cosrep, sinrep=sinrep,
                permQ=permQ.astype(BF),
                zcol=np.zeros((128, 1), np.float32),
                zrow=np.zeros((1, 512), BF))


def build_program(upto=None):
    upto = upto or _UPTO
    lvl = _ORDER.index(upto)
    nc = bass.Bass("TRN2", target_bir_lowering=False)

    x_d = nc.dram_tensor("x", [S, C], F32, kind="ExternalInput").ap()
    out_d = nc.dram_tensor("out", [S, C], F32, kind="ExternalOutput").ap()
    wq_d = nc.dram_tensor("wq", [C, C], F32, kind="ExternalInput").ap()
    wk_d = nc.dram_tensor("wk", [C, 256], F32, kind="ExternalInput").ap()
    wv_d = nc.dram_tensor("wv", [C, 256], F32, kind="ExternalInput").ap()
    wo_d = nc.dram_tensor("wo", [C, C], F32, kind="ExternalInput").ap()
    wrT_d = nc.dram_tensor("wrT", [C, C], F32, kind="ExternalInput").ap()
    wkT_d = nc.dram_tensor("wkT", [C, C], F32, kind="ExternalInput").ap()
    wvT_d = nc.dram_tensor("wvT", [C, C], F32, kind="ExternalInput").ap()
    bqT_d = nc.dram_tensor("bqT", [128, 8], F32, kind="ExternalInput").ap()
    bkT_d = nc.dram_tensor("bkT", [128, 2], F32, kind="ExternalInput").ap()
    bv_d = nc.dram_tensor("bv", [1, 256], F32, kind="ExternalInput").ap()
    bo_d = nc.dram_tensor("bo", [1, C], F32, kind="ExternalInput").ap()
    muR_d = nc.dram_tensor("muR", [128, 8], F32, kind="ExternalInput").ap()
    muK_d = nc.dram_tensor("muK", [128, 8], F32, kind="ExternalInput").ap()
    ident_d = nc.dram_tensor("ident", [128, 128], BF16, kind="ExternalInput").ap()
    tri_d = nc.dram_tensor("tri", [128, 128], BF16, kind="ExternalInput").ap()
    onescol_d = nc.dram_tensor("onescol", [128, 128], BF16, kind="ExternalInput").ap()
    pat2_d = nc.dram_tensor("pat2", [2, 128], F32, kind="ExternalInput").ap()
    pat2T_d = nc.dram_tensor("pat2T", [128, 2], BF16, kind="ExternalInput").ap()
    ones1f_d = nc.dram_tensor("ones1f", [1, 128], F32, kind="ExternalInput").ap()
    epst_d = nc.dram_tensor("epst", [128, 1], F32, kind="ExternalInput").ap()
    zcol_d = nc.dram_tensor("zcol", [128, 1], F32, kind="ExternalInput").ap()
    zrow_d = nc.dram_tensor("zrow", [1, 512], BF16, kind="ExternalInput").ap()
    cosrep_d = nc.dram_tensor("cosrep", [128, S], BF16, kind="ExternalInput").ap()
    sinrep_d = nc.dram_tensor("sinrep", [128, S], BF16, kind="ExternalInput").ap()
    permQ_d = nc.dram_tensor("permQ", [128, 128], BF16, kind="ExternalInput").ap()
    xnT_h = nc.dram_tensor("xnT_h", [C, S], BF16).ap()
    attn_h = nc.dram_tensor("attn_h", [C, S], BF16).ap()
    den_h = nc.dram_tensor("den_h", [128, 512], F32).ap()
    rden_h = nc.dram_tensor("rden_h", [128, 512], F32).ap()

    base_ar = nc.auto_arena()
    s_ident = base_ar.sbuf_tensor("s_ident", [128, 128], BF16)
    s_tri = base_ar.sbuf_tensor("s_tri", [128, 128], BF16)
    s_ones = base_ar.sbuf_tensor("s_ones", [128, 128], BF16)
    s_pat2 = base_ar.sbuf_tensor("s_pat2", [2, 128], F32)
    s_pat2T = base_ar.sbuf_tensor("s_pat2T", [128, 2], BF16)
    s_ones1f = base_ar.sbuf_tensor("s_ones1f", [1, 128], F32)
    s_eps = base_ar.sbuf_tensor("s_eps", [128, 1], F32)
    s_zc = base_ar.sbuf_tensor("s_zc", [128, 1], F32)
    s_zrow = base_ar.sbuf_tensor("s_zrow", [1, 512], BF16)
    s_bqT = base_ar.sbuf_tensor("s_bqT", [128, 8], F32)
    s_bkT = base_ar.sbuf_tensor("s_bkT", [128, 2], F32)
    s_bv16 = base_ar.sbuf_tensor("s_bv16", [1, 256], BF16)
    s_bo = base_ar.sbuf_tensor("s_bo", [1, C], F32)
    s_muR = base_ar.sbuf_tensor("s_muR", [128, 8], F32)
    s_muK = base_ar.sbuf_tensor("s_muK", [128, 8], F32)

    qkv_ar = nc.auto_arena()
    Qbuf = [qkv_ar.sbuf_tensor(f"Qbuf{g}", [128, S], BF16) for g in range(8)]
    Kpk = [qkv_ar.sbuf_tensor(f"Kpk{h}", [128, S], BF16) for h in range(2)]
    Vbuf = qkv_ar.sbuf_tensor("Vbuf", [128, NKV * NT * 65], BF16)
    s_kabs = qkv_ar.sbuf_tensor("s_kabs", [128, 2], F32)
    s_perm = qkv_ar.sbuf_tensor("s_perm", [128, 128], BF16)

    A_ar = nc.auto_arena()
    s_xt = A_ar.sbuf_tensor("s_xt", [128, 2 * C], F32)
    s_stats = A_ar.sbuf_tensor("s_stats", [128, 12], F32)
    s_mv = A_ar.sbuf_tensor("s_mv", [128, 4], F32)
    s_lnt = A_ar.sbuf_tensor("s_lnt", [128, 2], F32)
    s_rstd = A_ar.sbuf_tensor("s_rstd", [128, 2], F32)
    s_negm = A_ar.sbuf_tensor("s_negm", [128, 2], F32)
    s_xn = A_ar.sbuf_tensor("s_xn", [128, 2 * C], BF16)
    s_xnT = A_ar.sbuf_tensor("s_xnT", [128, 2 * 8 * 512], BF16)
    s_ktmp = A_ar.sbuf_tensor("s_ktmp", [128, 1024], BF16)
    s_wq16 = A_ar.sbuf_tensor("s_wq16", [128, 8 * C], BF16)
    s_wk16 = A_ar.sbuf_tensor("s_wk16", [128, 8 * 256], BF16)
    s_wv16 = A_ar.sbuf_tensor("s_wv16", [128, 8 * 256], BF16)

    A2_ar = nc.auto_arena()
    s_cos = A2_ar.sbuf_tensor("s_cos", [128, S], BF16)
    s_sin = A2_ar.sbuf_tensor("s_sin", [128, S], BF16)
    s_rtmp = A2_ar.sbuf_tensor("s_rtmp", [128, 512], BF16)
    s_kacc = A2_ar.sbuf_tensor("s_kacc", [128, 1024], BF16)

    cw_ar = nc.auto_arena()
    s_wo16 = cw_ar.sbuf_tensor("s_wo16", [128, 8 * C], BF16)
    s_wrT = cw_ar.sbuf_tensor("s_wrT", [128, 8 * C], BF16)
    s_wkT = cw_ar.sbuf_tensor("s_wkT", [128, 8 * C], BF16)
    s_wvT = cw_ar.sbuf_tensor("s_wvT", [128, 8 * C], BF16)

    R_ar = nc.auto_arena()
    s_dsb = R_ar.sbuf_tensor("s_dsb", [128, 512], F32)
    s_rsb = R_ar.sbuf_tensor("s_rsb", [128, 512], F32)

    B_ar = nc.auto_arena()
    s_P = B_ar.sbuf_tensor("s_P", [128, 2 * 1536], BF16)
    s_qmx = B_ar.sbuf_tensor("s_qmx", [128, 8], BF16)
    s_prod = B_ar.sbuf_tensor("s_prod", [128, 8], BF16)
    s_sc = B_ar.sbuf_tensor("s_sc", [1, 64], BF16)
    s_btA = B_ar.sbuf_tensor("s_btA", [128, 64], F32)
    s_num = B_ar.sbuf_tensor("s_num", [128, 1024], BF16)
    s_den = B_ar.sbuf_tensor("s_den", [128, 1024], F32)

    C_ar = nc.auto_arena()
    s_xnTc = C_ar.sbuf_tensor("s_xnTc", [128, 2 * 8 * 513], BF16)
    s_attc = C_ar.sbuf_tensor("s_attc", [128, 2 * 8 * 512], BF16)
    s_xr = C_ar.sbuf_tensor("s_xr", [128, 2 * 8 * 512], BF16)
    s_xk = C_ar.sbuf_tensor("s_xk", [128, 2 * 8 * 512], BF16)
    s_dt = C_ar.sbuf_tensor("s_dt", [128, 512], BF16)
    s_rnat = C_ar.sbuf_tensor("s_rnat", [128, 4 * C], BF16)
    s_krelu = C_ar.sbuf_tensor("s_krelu", [128, 512], BF16)
    s_kact = C_ar.sbuf_tensor("s_kact", [128, 2 * 8 * 512], BF16)
    s_xc = C_ar.sbuf_tensor("s_xc", [128, 2 * C], F32)
    s_bob = C_ar.sbuf_tensor("s_bob", [128, C], F32)
    s_rd = C_ar.sbuf_tensor("s_rd", [2, 8 * 512], F32)
    s_f1 = C_ar.sbuf_tensor("s_f1", [128, 2 * 512], F32)
    s_out = C_ar.sbuf_tensor("s_out", [128, 2 * C], F32)

    from contextlib import ExitStack
    ctx = ExitStack()

    def sem(name):
        return ctx.enter_context(nc.semaphore(name))

    LDS = sem("LDS")
    WL = sem("WL")
    LDx = [sem(f"LDx{i}") for i in range(2)]
    LDd = sem("LDd")
    SX = [sem(f"SX{i}") for i in range(2)]
    SB = sem("SB")
    SR = sem("SR")
    LCxn = [sem(f"LCxn{i}") for i in range(2)]
    LCat = [sem(f"LCat{i}") for i in range(2)]
    LCx = [sem(f"LCx{i}") for i in range(2)]
    LCrd = [sem(f"LCrd{i}") for i in range(2)]
    SO = [sem(f"SO{i}") for i in range(2)]
    PEs = sem("PEs")
    ACs = sem("ACs")
    DVs = sem("DVs")
    n = dict(PE=0, AC=0, DV=0)

    block = ctx.enter_context(nc.Block())

    base_ar.allocate(side="left")
    qkv_ar.allocate(side="left")
    A_ar.allocate(side="left")
    A2_ar.allocate(side="right")

    NSETUP = 17

    @block.sync
    def _(e):
        for dst, src in [(s_ident, ident_d), (s_tri, tri_d), (s_ones, onescol_d),
                         (s_pat2, pat2_d), (s_pat2T, pat2T_d),
                         (s_ones1f, ones1f_d), (s_eps, epst_d),
                         (s_zc, zcol_d), (s_zrow, zrow_d), (s_bqT, bqT_d), (s_bkT, bkT_d),
                         (s_bo, bo_d), (s_muR, muR_d), (s_muK, muK_d),
                         (s_cos, cosrep_d), (s_sin, sinrep_d), (s_perm, permQ_d)]:
            e.dma_start(out=dst[:, :], in_=src).then_inc(LDS, 16)

    @block.gpsimd
    def _(e):
        e.dma_start(out=s_wq16[:, :].rearrange("p (cc d) -> p cc d", d=C),
                    in_=wq_d.rearrange("(cc p) d -> p cc d", p=128)).then_inc(WL, 16)
        e.dma_start(out=s_wk16[:, :].rearrange("p (cc d) -> p cc d", d=256),
                    in_=wk_d.rearrange("(cc p) d -> p cc d", p=128)).then_inc(WL, 16)
        e.dma_start(out=s_wv16[:, :].rearrange("p (cc d) -> p cc d", d=256),
                    in_=wv_d.rearrange("(cc p) d -> p cc d", p=128)).then_inc(WL, 16)
        e.dma_start(out=s_bv16[:, :], in_=bv_d).then_inc(WL, 16)
    WL_A = 64

    @block.vector
    def _(e):
        e.wait_ge(LDS, 16 * NSETUP)
        ap = bass.AP(Vbuf, 64, [[Vbuf.shape[1], 128], [65, NKV * NT]])
        e.tensor_copy(ap, s_ones[:, 0:NKV * NT]).then_inc(DVs, 1)
        n['DV'] += 1

    psA = ExitStack()
    ptrA = psA.enter_context(nc.psum_tensor("ptrA", [128, 512], F32))
    ptrB = psA.enter_context(nc.psum_tensor("ptrB", [128, 512], F32))
    pq = psA.enter_context(nc.psum_tensor("pq", [128, 1024], F32))
    pk = psA.enter_context(nc.psum_tensor("pk", [128, 512], F32))
    pv = psA.enter_context(nc.psum_tensor("pv", [128, 512], F32))
    prp = [psA.enter_context(nc.psum_tensor(f"prp{i}", [128, 512], F32))
           for i in range(2)]

    ACH = {}
    nA = min(NCH, _NCHA)
    for ch in range(nA):
        b = ch % 2
        pe0, ac0, dv0 = n['PE'], n['AC'], n['DV']
        stats = [dv0 + 2 * i + 1 for i in range(4)]
        negm = [dv0 + 2 * i + 2 for i in range(4)]
        vcopy = [dv0 + 8 + t + 1 for t in range(4)]
        rope_dv = [dv0 + 12 + u + 1 for u in range(10)]
        rstd_c = [ac0 + 4 * i + 1 for i in range(4)]
        xn_c = [ac0 + 4 * i + 2 for i in range(4)]
        evacA = [ac0 + 4 * i + 3 for i in range(4)]
        evacB = [ac0 + 4 * i + 4 for i in range(4)]
        qevac = [ac0 + 16 + g + 1 for g in range(8)]
        kevac = [ac0 + 24 + g2 + 1 for g2 in range(2)]
        trA = [pe0 + 2 * i + 1 for i in range(4)]
        trB = [pe0 + 2 * i + 2 for i in range(4)]
        qmm = [pe0 + 8 + g + 1 for g in range(8)]
        kmm = [pe0 + 16 + g2 + 1 for g2 in range(2)]
        vmm = [pe0 + 18 + t + 1 for t in range(4)]
        rope_pe = [pe0 + 22 + u + 1 for u in range(10)]
        ACH[ch] = dict(xn=xn_c, vmm=vmm, evacA=evacA, evacB=evacB,
                       vcopy=vcopy, qevac=qevac, kevac=kevac, trB=trB,
                       rope=rope_dv, rope_pe=rope_pe)

        @block.sync
        def _(e, ch=ch):
            for i in range(4):
                st = 4 * ch + i
                if st >= 2:
                    e.wait_ge(ACs, ACH[(st - 2) // 4]['xn'][(st - 2) % 4])
                e.dma_start(out=s_xt[:, (st % 2) * C:(st % 2 + 1) * C],
                            in_=x_d[st * 128:(st + 1) * 128, :]
                            ).then_inc(LDx[st % 2], 16)

        @block.vector
        def _(e, ch=ch, b=b, vmm=vmm, rstd_c=rstd_c):
            for i in range(4):
                st = 4 * ch + i
                sl = st % 2
                e.wait_ge(LDx[sl], 16 * (st // 2 + 1))
                xt = s_xt[:, sl * C:(sl + 1) * C]
                e.bn_stats(s_stats[:, 0:6], xt[:, 0:512])
                e.bn_stats(s_stats[:, 6:12], xt[:, 512:1024])
                e.drain()
                e.bn_aggr(s_mv[:, sl * 2:sl * 2 + 2], s_stats[:, :])
                e.drain().then_inc(DVs, 1)
                e.wait_ge(ACs, rstd_c[i])
                e.tensor_scalar(s_negm[:, sl:sl + 1], s_mv[:, sl * 2:sl * 2 + 1],
                                s_rstd[:, sl:sl + 1], -1.0,
                                op0=ALU.mult, op1=ALU.mult).then_inc(DVs, 1)
            for t in range(4):
                e.wait_ge(PEs, vmm[t])
                st = 4 * ch + t
                for kv in range(4):
                    ins = e.tensor_copy(
                        Vbuf[:, (kv * NT + st) * 65:(kv * NT + st) * 65 + 64],
                        pv[:, kv * 64:(kv + 1) * 64])
                ins.then_inc(DVs, 1)
            cosl = s_cos[:, ch * 512:(ch + 1) * 512]
            sinl = s_sin[:, ch * 512:(ch + 1) * 512]
            for u in range(10):
                bank = prp[u % 2]
                e.wait_ge(PEs, rope_pe[u])
                if u < 8:
                    tgt = Qbuf[u][:, ch * 512:(ch + 1) * 512]
                    e.tensor_tensor(s_rtmp[:, :], bank[:, :], sinl, op=ALU.mult)
                    e.tensor_tensor(tgt, tgt, cosl, op=ALU.mult)
                    e.tensor_tensor(tgt, tgt, s_rtmp[:, :],
                                    op=ALU.add).then_inc(DVs, 1)
                else:
                    g2 = u - 8
                    tgt = Kpk[g2][:, ch * 512:(ch + 1) * 512]
                    acc = s_kacc[:, g2 * 512:(g2 + 1) * 512]
                    e.tensor_tensor(s_rtmp[:, :], bank[:, :], sinl, op=ALU.mult)
                    e.tensor_tensor(tgt, s_ktmp[:, g2 * 512:(g2 + 1) * 512],
                                    cosl, op=ALU.mult)
                    e.tensor_tensor(tgt, tgt, s_rtmp[:, :], op=ALU.add)
                    e.tensor_scalar(s_rtmp[:, :], tgt, -1.0, None, op0=ALU.mult)
                    if ch == 0:
                        e.tensor_tensor(acc, tgt, s_rtmp[:, :],
                                        op=ALU.max).then_inc(DVs, 1)
                    else:
                        e.tensor_tensor(acc, acc, tgt, op=ALU.max)
                        e.tensor_tensor(acc, acc, s_rtmp[:, :],
                                        op=ALU.max).then_inc(DVs, 1)
        n['DV'] += 22

        @block.scalar
        def _(e, ch=ch, b=b, stats=stats, negm=negm, trA=trA, trB=trB,
              qmm=qmm, kmm=kmm):
            for i in range(4):
                st = 4 * ch + i
                sl = st % 2
                e.wait_ge(DVs, stats[i])
                e.activation(s_lnt[:, sl:sl + 1], s_mv[:, sl * 2 + 1:sl * 2 + 2],
                             AF.Ln, bias=s_eps[:, :])
                e.activation(s_rstd[:, sl:sl + 1], s_lnt[:, sl:sl + 1],
                             AF.Exp, scale=-0.5).then_inc(ACs, 1)
                e.wait_ge(DVs, negm[i])
                if st >= 2:
                    e.wait_ge(PEs, ACH[(st - 2) // 4]['trB'][(st - 2) % 4])
                e.activation(s_xn[:, sl * C:(sl + 1) * C], s_xt[:, sl * C:(sl + 1) * C],
                             AF.Identity, bias=s_negm[:, sl:sl + 1],
                             scale=s_rstd[:, sl:sl + 1]).then_inc(ACs, 1)
                e.wait_ge(PEs, trA[i])
                if ch >= 2 and i == 0:
                    e.wait_ge(PEs, ACH[ch - 2]['vmm'][3])
                    e.wait_ge(SX[b], 16 * 8 * (ch // 2))
                for g in range(4):
                    ins = e.activation(
                        s_xnT[:, (b * 8 + g) * 512 + i * 128:(b * 8 + g) * 512 + (i + 1) * 128],
                        ptrA[:, g * 128:(g + 1) * 128], AF.Identity)
                ins.then_inc(ACs, 1)
                e.wait_ge(PEs, trB[i])
                for g in range(4):
                    ins = e.activation(
                        s_xnT[:, (b * 8 + 4 + g) * 512 + i * 128:(b * 8 + 4 + g) * 512 + (i + 1) * 128],
                        ptrB[:, g * 128:(g + 1) * 128], AF.Identity)
                ins.then_inc(ACs, 1)
            for g in range(8):
                e.wait_ge(PEs, qmm[g])
                e.activation(Qbuf[g][:, ch * 512:(ch + 1) * 512],
                             pq[:, (g % 2) * 512:(g % 2 + 1) * 512],
                             AF.Identity, bias=s_bqT[:, g:g + 1]).then_inc(ACs, 1)
            for g2 in range(2):
                e.wait_ge(PEs, kmm[g2])
                if ch >= 1:
                    e.wait_ge(DVs, ACH[ch - 1]['rope'][8 + g2])
                e.activation(s_ktmp[:, g2 * 512:(g2 + 1) * 512], pk[:, :],
                             AF.Identity, bias=s_bkT[:, g2:g2 + 1]).then_inc(ACs, 1)
        n['AC'] += 26

        @block.gpsimd
        def _(e, ch=ch, b=b, evacB=evacB):
            e.wait_ge(ACs, evacB[3])
            for g in range(8):
                e.dma_start(out=xnT_h[g * 128:(g + 1) * 128, ch * 512:(ch + 1) * 512],
                            in_=s_xnT[:, (b * 8 + g) * 512:(b * 8 + g + 1) * 512]
                            ).then_inc(SX[b], 16)

        @block.tensor
        def _(e, ch=ch, b=b, xn_c=xn_c, evacA=evacA, evacB=evacB,
              qevac=qevac, kevac=kevac, vcopy=vcopy):
            for i in range(4):
                sl = (4 * ch + i) % 2
                e.wait_ge(ACs, xn_c[i])
                if i > 0:
                    e.wait_ge(ACs, evacA[i - 1])
                elif ch > 0:
                    e.wait_ge(ACs, ACH[ch - 1]['evacA'][3])
                for g in range(4):
                    ins = e.matmul(ptrA[:, g * 128:(g + 1) * 128],
                                   s_xn[:, sl * C + g * 128:sl * C + (g + 1) * 128],
                                   s_ident[:, :], start=(g == 0), stop=(g == 3))
                ins.then_inc(PEs, 1)
                if i > 0:
                    e.wait_ge(ACs, evacB[i - 1])
                elif ch > 0:
                    e.wait_ge(ACs, ACH[ch - 1]['evacB'][3])
                for g in range(4):
                    ins = e.matmul(ptrB[:, g * 128:(g + 1) * 128],
                                   s_xn[:, sl * C + (4 + g) * 128:sl * C + (5 + g) * 128],
                                   s_ident[:, :], start=(g == 0), stop=(g == 3))
                ins.then_inc(PEs, 1)
            e.wait_ge(ACs, evacB[3])
            if ch == 0:
                e.wait_ge(WL, WL_A)
            for g in range(8):
                if g >= 2:
                    e.wait_ge(ACs, qevac[g - 2])
                elif ch > 0:
                    e.wait_ge(ACs, ACH[ch - 1]['qevac'][6 + g])
                for cc in range(8):
                    ins = e.matmul(pq[:, (g % 2) * 512:(g % 2 + 1) * 512],
                                   s_wq16[:, cc * C + g * 128:cc * C + (g + 1) * 128],
                                   s_xnT[:, (b * 8 + cc) * 512:(b * 8 + cc + 1) * 512],
                                   start=(cc == 0), stop=(cc == 7))
                ins.then_inc(PEs, 1)
            for g2 in range(2):
                if g2 >= 1:
                    e.wait_ge(ACs, kevac[0])
                elif ch > 0:
                    e.wait_ge(ACs, ACH[ch - 1]['kevac'][1])
                for cc in range(8):
                    ins = e.matmul(pk[:, :],
                                   s_wk16[:, cc * 256 + g2 * 128:cc * 256 + (g2 + 1) * 128],
                                   s_xnT[:, (b * 8 + cc) * 512:(b * 8 + cc + 1) * 512],
                                   start=(cc == 0), stop=(cc == 7))
                ins.then_inc(PEs, 1)
            for t in range(4):
                if t >= 1:
                    e.wait_ge(DVs, vcopy[t - 1])
                elif ch >= 1:
                    e.wait_ge(DVs, ACH[ch - 1]['vcopy'][3])
                for cc in range(8):
                    e.matmul(pv[:, 0:256],
                             s_xnT[:, (b * 8 + cc) * 512 + t * 128:(b * 8 + cc) * 512 + (t + 1) * 128],
                             s_wv16[:, cc * 256:(cc + 1) * 256],
                             start=(cc == 0), stop=False)
                e.matmul(pv[:, 0:256], s_ones[0:1, 0:128], s_bv16[:, :],
                         start=False, stop=True).then_inc(PEs, 1)
            for u in range(10):
                if u >= 2:
                    e.wait_ge(DVs, rope_dv[u - 2])
                elif ch > 0:
                    e.wait_ge(DVs, ACH[ch - 1]['rope'][8 + u])
                if u < 8:
                    e.wait_ge(ACs, qevac[u])
                    src = Qbuf[u][:, ch * 512:(ch + 1) * 512]
                else:
                    e.wait_ge(ACs, kevac[u - 8])
                    src = s_ktmp[:, (u - 8) * 512:(u - 7) * 512]
                e.matmul(prp[u % 2][:, :], s_perm[:, :], src,
                         start=True, stop=True).then_inc(PEs, 1)
        n['PE'] += 32

    @block.vector
    def _(e):
        for g2 in range(2):
            e.reduce_max(s_kabs[:, g2:g2 + 1],
                         s_kacc[:, g2 * 512:(g2 + 1) * 512], axis=AX.X)
        e.drain()
        e.tensor_scalar(s_kabs[:, 0:2], s_kabs[:, 0:2], 0.125, None,
                        op0=ALU.mult).then_inc(DVs, 1)
    n['DV'] += 1
    A_done = dict(n)

    store_state = dict(SXa=16 * 8 * ((nA + 1) // 2), SXb=16 * 8 * (nA // 2),
                       SB=0, SR=0, SOa=0, SOb=0)

    def store_totals():
        return [(SX[0], store_state['SXa']), (SX[1], store_state['SXb']),
                (SB, store_state['SB']), (SR, store_state['SR']),
                (SO[0], store_state['SOa']), (SO[1], store_state['SOb'])]

    def _finish(open_psums, open_left, open_right):
        @block.scalar
        def _(e):
            for s, tot in store_totals():
                if tot:
                    e.wait_ge(s, tot)
        for ps in open_psums:
            ps.close()
        for ar in open_left:
            ar.free()
        for ar in open_right:
            ar.free()
        base_ar.free()
        ctx.close()
        return nc

    if lvl == 0:
        return _finish([psA], [A_ar, qkv_ar], [A2_ar])

    A2_dv = n['DV']

    psA.close()
    A_ar.free()
    A2_ar.free()
    cw_ar.allocate(side="right")
    B_ar.allocate(side="left")

    if lvl == 1:
        return _finish([], [B_ar, qkv_ar], [cw_ar])

    @block.gpsimd
    def _(e):
        e.wait_ge(DVs, A2_dv)
        e.dma_start(out=s_wo16[:, :].rearrange("p (g d) -> p g d", d=C),
                    in_=wo_d.rearrange("(g p) d -> p g d", p=128)).then_inc(WL, 16)
        for dst, src in [(s_wrT, wrT_d), (s_wkT, wkT_d), (s_wvT, wvT_d)]:
            e.dma_start(out=dst[:, :].rearrange("p (cc d) -> p cc d", d=C),
                        in_=src.rearrange("(cc p) d -> p cc d", p=128)).then_inc(WL, 16)
    WL_ALL = WL_A + 64

    psB = ExitStack()
    pS = psB.enter_context(nc.psum_tensor("pS", [128, 3072], F32))
    pAV = psB.enter_context(nc.psum_tensor("pAV", [128, 1024], F32))

    @block.tensor
    def _(e):
        e.wait_ge(LDS, 16 * NSETUP)
        e.wait_ge(PEs, A_done['PE'])
        for k2 in range(6):
            ins = e.matmul(pS[:, k2 * 512:(k2 + 1) * 512], s_ones[0:1, :],
                           s_zrow[:, :], start=True, stop=True)
        ins.then_inc(PEs, 1)
    n['PE'] += 1

    bq_dv0 = n['DV']
    bq_pe0 = n['PE']

    @block.vector
    def _(e):
        e.wait_ge(PEs, A_done['PE'])
        e.wait_ge(ACs, A_done['AC'])
        for pr in range(8):
            cpx = pr // 4
            for qt in range(NCH):
                e.reduce_max(s_qmx[:, qt:qt + 1],
                             Qbuf[pr][:, qt * 512:(qt + 1) * 512], axis=AX.X,
                             apply_absolute_value=True)
            e.drain()
            ins = None
            for qt in range(NCH):
                ins = e.tensor_scalar(s_prod[:, qt:qt + 1], s_qmx[:, qt:qt + 1],
                                      s_kabs[:, cpx:cpx + 1], -1.0,
                                      op0=ALU.mult, op1=ALU.mult)
            ins.then_inc(DVs, 1)
            e.wait_ge(PEs, bq_pe0 + pr + 1)
            e.tensor_copy(s_sc[0:1, pr * 8:(pr + 1) * 8],
                          pAV[0:1, 0:8]).then_inc(DVs, 1)
        e.wait_ge(PEs, bq_pe0 + 9)
        e.tensor_copy(s_btA[:, 0:64], pAV[:, 512:576]).then_inc(DVs, 1)
    n['DV'] += 17
    dv_btA = n['DV']

    @block.tensor
    def _(e):
        for pr in range(8):
            e.wait_ge(DVs, bq_dv0 + 2 * pr + 1)
            if pr > 0:
                e.wait_ge(DVs, bq_dv0 + 2 * pr)
            e.matmul(pAV[0:1, 0:8], s_pat2T[:, 0:1], s_prod[:, 0:8],
                     start=True, stop=True).then_inc(PEs, 1)
        e.wait_ge(DVs, bq_dv0 + 16)
        e.matmul(pAV[:, 512:576], s_ones[0:1, :], s_sc[0:1, 0:64],
                 start=True, stop=True).then_inc(PEs, 1)
    n['PE'] += 9

    last_exp = 0
    avevac_prev = 0
    pair_serial = 0

    for qt in range(NCH):
        for pr in range(8):
            cpr = pr // 4
            nfull = 4 * qt
            units = [('f', t, hh) for t in range(nfull) for hh in range(2)] \
                  + [('d', kk, hh) for kk in range(4) for hh in range(2)]
            nu = len(units)
            batches = [units[i:i + 3] for i in range(0, nu, 3)]
            nb = len(batches)
            pe0, ac0, dv0 = n['PE'], n['AC'], n['DV']

            pe_scores = [pe0 + 1]
            for i in range(1, nb):
                pe_scores.append(pe0 + 2 * i)
            pe_avs = [pe0 + 2 * i + 3 for i in range(nb - 1)]
            pe_avs.append(pe0 + 2 * nb)
            ac_exp = [ac0 + 1 + i for i in range(nb)]
            dv_masks = {}
            dvc = dv0
            for i, bt in enumerate(batches):
                if any(u[0] == 'd' for u in bt):
                    dvc += 1
                    dv_masks[i] = dvc
            dv_avevac = dvc + 1
            par = pair_serial % 2

            @block.vector
            def _(e, qt=qt, pr=pr, batches=batches, pe_avs=pe_avs,
                  ac_exp=ac_exp, dv_masks=dv_masks, pair_serial=pair_serial):
                for i, bt in enumerate(batches):
                    if not any(u[0] == 'd' for u in bt):
                        continue
                    e.wait_ge(ACs, ac_exp[i])
                    ins = None
                    for pos, u in enumerate(bt):
                        if u[0] != 'd':
                            continue
                        kk = u[1]
                        base = (i % 2) * 1536 + pos * 512 + kk * 128
                        ins = e.tensor_tensor(s_P[:, base:base + 128],
                                              s_P[:, base:base + 128],
                                              s_tri[:, :], op=ALU.mult)
                    ins.then_inc(DVs, 1)
                e.wait_ge(PEs, pe_avs[-1])
                if pair_serial >= 1:
                    e.wait_ge(SB, 64 * pair_serial)
                e.tensor_copy(s_num[0:64, 0:512], pAV[0:64, 0:512])
                e.tensor_copy(s_num[0:64, 512:1024], pAV[0:64, 512:1024])
                e.tensor_copy(s_den[64:65, 0:512], pAV[64:65, 0:512])
                ins = e.tensor_copy(s_den[64:65, 512:1024], pAV[64:65, 512:1024])
                ins.then_inc(DVs, 1)
            n['DV'] = dv_avevac

            @block.scalar
            def _(e, qt=qt, pr=pr, batches=batches,
                  pe_scores=pe_scores):
                for i, bt in enumerate(batches):
                    e.wait_ge(PEs, pe_scores[i])
                    if i == 0 and qt == 0:
                        e.wait_ge(DVs, dv_btA)
                    w = len(bt) * 512
                    e.activation(s_P[:, (i % 2) * 1536:(i % 2) * 1536 + w],
                                 pS[:, (i % 2) * 1536:(i % 2) * 1536 + w],
                                 AF.Exp, scale=0.125,
                                 bias=s_btA[:, pr * 8 + qt:pr * 8 + qt + 1]
                                 ).then_inc(ACs, 1)
            n['AC'] = ac_exp[-1]

            @block.gpsimd
            def _(e, qt=qt, pr=pr, dv_avevac=dv_avevac):
                e.wait_ge(DVs, dv_avevac)
                he, ho = 2 * pr, 2 * pr + 1
                e.dma_start(out=attn_h[he * 64:(he + 1) * 64, qt * 512:(qt + 1) * 512],
                            in_=s_num[0:64, 0:512]).then_inc(SB, 16)
                e.dma_start(out=attn_h[ho * 64:(ho + 1) * 64, qt * 512:(qt + 1) * 512],
                            in_=s_num[0:64, 512:1024]).then_inc(SB, 16)
                e.dma_start(out=den_h[qt * 16 + he:qt * 16 + he + 1, :],
                            in_=s_den[64:65, 0:512]).then_inc(SB, 16)
                e.dma_start(out=den_h[qt * 16 + ho:qt * 16 + ho + 1, :],
                            in_=s_den[64:65, 512:1024]).then_inc(SB, 16)

            @block.tensor
            def _(e, qt=qt, pr=pr, cpr=cpr, batches=batches,
                  last_exp=last_exp, avevac_prev=avevac_prev,
                  dv_masks=dv_masks, ac_exp=ac_exp, nb=nb,
                  pair_serial=pair_serial):
                e.wait_ge(ACs, last_exp)
                started = {0: False, 1: False}

                def emit_scores(i, bt):
                    if i >= 2:
                        e.wait_ge(ACs, ac_exp[i - 2])
                    ins = None
                    for pos, u in enumerate(bt):
                        hh = u[2]
                        base = 64 * hh
                        slot = (i % 2) * 1536 + pos * 512
                        if u[0] == 'f':
                            t = u[1]
                            ins = e.matmul(pS[:, slot:slot + 512],
                                           Kpk[cpr][base:base + 64, t * 128:(t + 1) * 128],
                                           Qbuf[pr][base:base + 64, qt * 512:(qt + 1) * 512],
                                           start=True, stop=True, tile_position=(base, 0))
                        else:
                            kk = u[1]
                            c0 = qt * 512 + kk * 128
                            ins = e.matmul(pS[:, slot + kk * 128:slot + 512],
                                           Kpk[cpr][base:base + 64, c0:c0 + 128],
                                           Qbuf[pr][base:base + 64, c0:(qt + 1) * 512],
                                           start=True, stop=True, tile_position=(base, 0))
                    ins.then_inc(PEs, 1)

                def emit_avs(i, bt):
                    if i == 0:
                        e.wait_ge(DVs, dv_btA if pair_serial == 0 else avevac_prev)
                    e.wait_ge(ACs, ac_exp[i])
                    if i in dv_masks:
                        e.wait_ge(DVs, dv_masks[i])
                    ins = None
                    for pos, u in enumerate(bt):
                        hh = u[2]
                        t = u[1] if u[0] == 'f' else qt * 4 + u[1]
                        slot = (i % 2) * 1536 + pos * 512
                        off = 0 if u[0] == 'f' else u[1] * 128
                        is_last = (u[0] == 'd' and u[1] == 3)
                        ins = e.matmul(pAV[0:65, hh * 512 + off:hh * 512 + 512],
                                       Vbuf[:, ((2 * cpr + hh) * NT + t) * 65:((2 * cpr + hh) * NT + t) * 65 + 65],
                                       s_P[:, slot + off:slot + 512],
                                       start=(not started[hh]), stop=is_last)
                        started[hh] = True
                    ins.then_inc(PEs, 1)

                emit_scores(0, batches[0])
                for i in range(1, nb):
                    emit_scores(i, batches[i])
                    emit_avs(i - 1, batches[i - 1])
                emit_avs(nb - 1, batches[nb - 1])
            n['PE'] = pe_avs[-1]
            last_exp = ac_exp[-1]
            avevac_prev = dv_avevac
            pair_serial += 1

    B_done = dict(n)
    store_state['SB'] = 64 * 64

    if lvl == 2:
        return _finish([psB], [B_ar, qkv_ar], [cw_ar])

    R_ar.allocate(side="right")

    @block.sync
    def _(e):
        e.wait_ge(SB, store_state['SB'])
        e.dma_start(out=s_dsb[:, :], in_=den_h).then_inc(LDd, 16)

    @block.vector
    def _(e):
        e.wait_ge(LDd, 16)
        e.reciprocal(s_rsb[:, :], s_dsb[:, :]).then_inc(DVs, 1)
    n['DV'] += 1
    dv_recip = n['DV']

    @block.gpsimd
    def _(e):
        e.wait_ge(DVs, dv_recip)
        e.dma_start(out=rden_h, in_=s_rsb[:, :]).then_inc(SR, 16)
    store_state['SR'] = 16

    psB.close()
    W_done = dict(n)

    if lvl == 3:
        return _finish([], [B_ar, qkv_ar], [R_ar, cw_ar])

    B_ar.free()
    qkv_ar.free()
    R_ar.free()
    C_ar.allocate(side="left")

    psC = ExitStack()
    pbc = psC.enter_context(nc.psum_tensor("pbc", [128, 512], F32))
    pkC = psC.enter_context(nc.psum_tensor("pkC", [128, 512], F32))
    prC = psC.enter_context(nc.psum_tensor("prC", [128, 1024], F32))
    pvC = psC.enter_context(nc.psum_tensor("pvC", [128, 1024], F32))
    pwC = psC.enter_context(nc.psum_tensor("pwC", [128, 1024], F32))

    @block.tensor
    def _(e):
        e.wait_ge(LDS, 16 * NSETUP)
        e.wait_ge(DVs, W_done['DV'])
        e.wait_ge(ACs, B_done['AC'])
        e.matmul(pwC[:, 0:512], s_ones1f[:, :], s_bo[:, 0:512], start=True, stop=True)
        e.matmul(pwC[:, 512:1024], s_ones1f[:, :], s_bo[:, 512:1024],
                 start=True, stop=True).then_inc(PEs, 1)
    n['PE'] += 1
    pe_bob = n['PE']

    @block.vector
    def _(e):
        e.wait_ge(PEs, pe_bob)
        e.tensor_copy(s_bob[:, :], pwC[:, :]).then_inc(DVs, 1)
    n['DV'] += 1
    dv_bob = n['DV']

    CCH = {}
    nC = min(NCH, _NCHC)
    for ch in range(nC):
        b = ch % 2
        pe0, ac0, dv0 = n['PE'], n['AC'], n['DV']
        dvc = dv0
        if ch == 0:
            dvc += 1
            mix_ct = []
            for g in range(8):
                dvc += 1
                mix_ct.append(dvc)
        else:
            mix_ct = CCH[ch - 1]['mixT']
        attnN_ct = []
        for g in range(8):
            dvc += 1
            attnN_ct.append(dvc)
        kact_ct = []
        for oc in range(8):
            dvc += 1
            kact_ct.append(dvc)
        xplus_ct = []
        final_ct = {}
        for t in range(4):
            dvc += 1
            xplus_ct.append(dvc)
            for dh in range(2):
                final_ct[(t, dh)] = dvc + 3
                dvc += 3
        mixT_ct = []
        if ch < min(NCH, _NCHC) - 1:
            for g in range(8):
                dvc += 1
                mixT_ct.append(dvc)
        pec = pe0
        bc_ct = []
        for g in range(8):
            pec += 1
            bc_ct.append(pec)
        kmm_ct = []
        for oc in range(8):
            pec += 1
            kmm_ct.append(pec)
        rmm_ct = {}
        vmm_ct = {}
        womm_ct = {}
        for t in range(4):
            for dh in range(2):
                pec += 1
                rmm_ct[(t, dh)] = pec
            for dh in range(2):
                pec += 1
                vmm_ct[(t, dh)] = pec
            for dh in range(2):
                pec += 1
                womm_ct[(t, dh)] = pec
        acc = ac0
        krelu_ct = []
        for oc in range(8):
            acc += 1
            krelu_ct.append(acc)
        sig_ct = {}
        for t in range(4):
            for dh in range(2):
                acc += 1
                sig_ct[(t, dh)] = acc
        CCH[ch] = dict(final=final_ct, mix=mix_ct, mixT=mixT_ct, attnN=attnN_ct,
                       kact=kact_ct, xplus=xplus_ct, sig=sig_ct, bc=bc_ct,
                       krelu=krelu_ct, last_pe=pec, last_dv=dvc)

        @block.sync
        def _(e, ch=ch, b=b):
            if ch == 0:
                e.wait_ge(PEs, W_done['PE'])
                e.wait_ge(DVs, W_done['DV'])
                e.wait_ge(ACs, B_done['AC'])
                e.wait_ge(SR, 16)
            for g in range(8):
                if ch >= 1:
                    e.wait_ge(PEs, CCH[ch - 1]['bc'][g])
                e.dma_start(out=s_rd[0:2, g * 512:(g + 1) * 512],
                            in_=rden_h[ch * 16 + 2 * g:ch * 16 + 2 * g + 2, :]
                            ).then_inc(LCrd[g % 2], 16)
            if ch >= 2:
                e.wait_ge(PEs, CCH[ch - 2]['last_pe'])
                e.wait_ge(DVs, CCH[ch - 2]['last_dv'])
            for g in range(8):
                if ch == 0:
                    dst = s_xnTc[:, (b * 8 + g) * 513 + 1:(b * 8 + g) * 513 + 513]
                    src = xnT_h[g * 128:(g + 1) * 128, 0:512]
                else:
                    dst = s_xnTc[:, (b * 8 + g) * 513:(b * 8 + g + 1) * 513]
                    src = xnT_h[g * 128:(g + 1) * 128, ch * 512 - 1:(ch + 1) * 512]
                e.dma_start(out=dst, in_=src).then_inc(LCxn[b], 16)
            for g in range(8):
                e.dma_start(out=s_attc[:, (b * 8 + g) * 512:(b * 8 + g + 1) * 512],
                            in_=attn_h[g * 128:(g + 1) * 128, ch * 512:(ch + 1) * 512]
                            ).then_inc(LCat[b], 16)

        @block.gpsimd
        def _(e, ch=ch, final_ct=final_ct):
            for t in range(4):
                tg = ch * 4 + t
                if tg >= 2:
                    prev_ch, prev_t = (tg - 2) // 4, (tg - 2) % 4
                    e.wait_ge(DVs, CCH[prev_ch]['final'][(prev_t, 1)])
                e.dma_start(out=s_xc[:, (tg % 2) * C:(tg % 2 + 1) * C],
                            in_=x_d[tg * 128:(tg + 1) * 128, :]
                            ).then_inc(LCx[tg % 2], 16)
            for t in range(4):
                tg = ch * 4 + t
                e.wait_ge(DVs, final_ct[(t, 1)])
                e.dma_start(out=out_d[tg * 128:(tg + 1) * 128, :],
                            in_=s_out[:, (tg % 2) * C:(tg % 2 + 1) * C]
                            ).then_inc(SO[tg % 2], 16)

        def emit_mix(e, tch):
            bt = tch % 2
            e.wait_ge(LCxn[bt], 16 * 8 * (tch // 2 + 1))
            if tch == 0:
                ins = None
                for g in range(8):
                    ap = s_xnTc[:, (bt * 8 + g) * 513:(bt * 8 + g) * 513 + 1]
                    ins = e.tensor_copy(ap, s_zc[:, :])
                ins.then_inc(DVs, 1)
            if tch >= 2:
                e.wait_ge(PEs, CCH[tch - 2]['last_pe'])
            for g in range(8):
                cur = s_xnTc[:, (bt * 8 + g) * 513 + 1:(bt * 8 + g) * 513 + 513]
                prev = s_xnTc[:, (bt * 8 + g) * 513:(bt * 8 + g) * 513 + 512]
                e.tensor_tensor(s_dt[:, :], cur, prev, op=ALU.subtract)
                e.scalar_tensor_tensor(s_xr[:, (bt * 8 + g) * 512:(bt * 8 + g + 1) * 512],
                                       s_dt[:, :], s_muR[:, g:g + 1], prev,
                                       op0=ALU.mult, op1=ALU.add)
                e.scalar_tensor_tensor(s_xk[:, (bt * 8 + g) * 512:(bt * 8 + g + 1) * 512],
                                       s_dt[:, :], s_muK[:, g:g + 1], prev,
                                       op0=ALU.mult, op1=ALU.add).then_inc(DVs, 1)

        @block.vector
        def _(e, ch=ch, b=b, bc_ct=bc_ct, vmm_ct=vmm_ct, womm_ct=womm_ct,
              krelu_ct=krelu_ct, sig_ct=sig_ct, emit_mix=emit_mix):
            if ch == 0:
                emit_mix(e, 0)
            e.wait_ge(LCat[b], 16 * 8 * (ch // 2 + 1))
            for g in range(8):
                e.wait_ge(PEs, bc_ct[g])
                e.tensor_tensor(s_attc[:, (b * 8 + g) * 512:(b * 8 + g + 1) * 512],
                                s_attc[:, (b * 8 + g) * 512:(b * 8 + g + 1) * 512],
                                pbc[:, :], op=ALU.mult).then_inc(DVs, 1)
            for oc in range(8):
                e.wait_ge(ACs, krelu_ct[oc])
                e.tensor_tensor(s_kact[:, (b * 8 + oc) * 512:(b * 8 + oc + 1) * 512],
                                s_krelu[:, :], s_krelu[:, :],
                                op=ALU.mult).then_inc(DVs, 1)
            for t in range(4):
                tg = ch * 4 + t
                e.wait_ge(LCx[tg % 2], 16 * (tg // 2 + 1))
                if ch == 0 and t == 0:
                    e.wait_ge(DVs, dv_bob)
                e.tensor_tensor(s_xc[:, (tg % 2) * C:(tg % 2 + 1) * C],
                                s_xc[:, (tg % 2) * C:(tg % 2 + 1) * C],
                                s_bob[:, :], op=ALU.add).then_inc(DVs, 1)
                for dh in range(2):
                    e.wait_ge(PEs, vmm_ct[(t, dh)])
                    e.wait_ge(ACs, sig_ct[(t, dh)])
                    e.tensor_tensor(s_f1[:, dh * 512:(dh + 1) * 512],
                                    s_rnat[:, t * C + dh * 512:t * C + (dh + 1) * 512],
                                    pvC[:, dh * 512:(dh + 1) * 512],
                                    op=ALU.mult).then_inc(DVs, 1)
                    e.wait_ge(PEs, womm_ct[(t, dh)])
                    e.tensor_tensor(s_f1[:, dh * 512:(dh + 1) * 512],
                                    s_f1[:, dh * 512:(dh + 1) * 512],
                                    pwC[:, dh * 512:(dh + 1) * 512],
                                    op=ALU.add).then_inc(DVs, 1)
                    if tg >= 2 and dh == 0:
                        e.wait_ge(SO[tg % 2], 16 * (tg // 2))
                    e.tensor_tensor(s_out[:, (tg % 2) * C + dh * 512:(tg % 2) * C + (dh + 1) * 512],
                                    s_f1[:, dh * 512:(dh + 1) * 512],
                                    s_xc[:, (tg % 2) * C + dh * 512:(tg % 2) * C + (dh + 1) * 512],
                                    op=ALU.add).then_inc(DVs, 1)
            if ch < nC - 1:
                emit_mix(e, ch + 1)
        n['DV'] = dvc

        @block.scalar
        def _(e, ch=ch, b=b, kmm_ct=kmm_ct, rmm_ct=rmm_ct, final_ct=final_ct):
            for oc in range(8):
                e.wait_ge(PEs, kmm_ct[oc])
                if oc >= 1:
                    e.wait_ge(DVs, CCH[ch]['kact'][oc - 1])
                elif ch >= 1:
                    e.wait_ge(DVs, CCH[ch - 1]['kact'][7])
                e.activation(s_krelu[:, :], pkC[:, :], AF.Relu).then_inc(ACs, 1)
            for t in range(4):
                for dh in range(2):
                    e.wait_ge(PEs, rmm_ct[(t, dh)])
                    if ch >= 1:
                        e.wait_ge(DVs, CCH[ch - 1]['final'][(t, dh)] - 2)
                    e.activation(s_rnat[:, t * C + dh * 512:t * C + (dh + 1) * 512],
                                 prC[:, dh * 512:(dh + 1) * 512],
                                 AF.Sigmoid).then_inc(ACs, 1)
        n['AC'] = acc

        @block.tensor
        def _(e, ch=ch, b=b, mix_ct=mix_ct, attnN_ct=attnN_ct,
              kact_ct=kact_ct, krelu_ct=krelu_ct, sig_ct=sig_ct,
              final_ct=final_ct):
            if ch == 0:
                e.wait_ge(WL, WL_ALL)
                e.wait_ge(DVs, W_done['DV'])
            for g in range(8):
                if g == 0:
                    e.wait_ge(LCrd[0], 16 * 4 * (ch + 1))
                    e.wait_ge(LCrd[1], 16 * 4 * (ch + 1))
                if g >= 1:
                    e.wait_ge(DVs, attnN_ct[g - 1])
                elif ch >= 1:
                    e.wait_ge(DVs, CCH[ch - 1]['attnN'][7])
                e.matmul(pbc[:, :], s_pat2[:, :],
                         s_rd[0:2, g * 512:(g + 1) * 512],
                         start=True, stop=True).then_inc(PEs, 1)
            e.wait_ge(DVs, mix_ct[7])
            for oc in range(8):
                if oc >= 1:
                    e.wait_ge(ACs, krelu_ct[oc - 1])
                elif ch >= 1:
                    e.wait_ge(ACs, CCH[ch - 1]['krelu'][7])
                for cc in range(8):
                    ins = e.matmul(pkC[:, :],
                                   s_wkT[:, cc * C + oc * 128:cc * C + (oc + 1) * 128],
                                   s_xk[:, (b * 8 + cc) * 512:(b * 8 + cc + 1) * 512],
                                   start=(cc == 0), stop=(cc == 7))
                ins.then_inc(PEs, 1)
            for t in range(4):
                for dh in range(2):
                    if t > 0:
                        e.wait_ge(ACs, sig_ct[(t - 1, dh)])
                    elif ch > 0:
                        e.wait_ge(ACs, CCH[ch - 1]['sig'][(3, dh)])
                    for cc in range(8):
                        ins = e.matmul(prC[:, dh * 512:(dh + 1) * 512],
                                       s_xr[:, (b * 8 + cc) * 512 + t * 128:(b * 8 + cc) * 512 + (t + 1) * 128],
                                       s_wrT[:, cc * C + dh * 512:cc * C + (dh + 1) * 512],
                                       start=(cc == 0), stop=(cc == 7))
                    ins.then_inc(PEs, 1)
                for dh in range(2):
                    e.wait_ge(DVs, kact_ct[7])
                    if t > 0:
                        e.wait_ge(DVs, final_ct[(t - 1, dh)] - 2)
                    elif ch > 0:
                        e.wait_ge(DVs, CCH[ch - 1]['final'][(3, dh)] - 2)
                    for oc in range(8):
                        ins = e.matmul(pvC[:, dh * 512:(dh + 1) * 512],
                                       s_kact[:, (b * 8 + oc) * 512 + t * 128:(b * 8 + oc) * 512 + (t + 1) * 128],
                                       s_wvT[:, oc * C + dh * 512:oc * C + (dh + 1) * 512],
                                       start=(oc == 0), stop=(oc == 7))
                    ins.then_inc(PEs, 1)
                for dh in range(2):
                    e.wait_ge(DVs, attnN_ct[7])
                    if t > 0:
                        e.wait_ge(DVs, final_ct[(t - 1, dh)] - 1)
                    elif ch > 0:
                        e.wait_ge(DVs, CCH[ch - 1]['final'][(3, dh)] - 1)
                    for g in range(8):
                        ins = e.matmul(pwC[:, dh * 512:(dh + 1) * 512],
                                       s_attc[:, (b * 8 + g) * 512 + t * 128:(b * 8 + g) * 512 + (t + 1) * 128],
                                       s_wo16[:, g * C + dh * 512:g * C + (dh + 1) * 512],
                                       start=(g == 0), stop=(g == 7))
                    ins.then_inc(PEs, 1)
        n['PE'] = pec

    store_state['SOa'] = 16 * ((4 * nC + 1) // 2)
    store_state['SOb'] = 16 * ((4 * nC) // 2)

    return _finish([psC], [C_ar], [cw_ar])


_CONSTS = None
_NC = None


def kernel(L0_attn_x, L0_attn_Wq, L0_attn_bq, L0_attn_Wk, L0_attn_bk,
           L0_attn_Wv, L0_attn_bv, L0_attn_Wo, L0_attn_bo,
           L0_ffn_W_r, L0_ffn_W_k, L0_ffn_W_v, L0_ffn_mu_r, L0_ffn_mu_k):
    global _CONSTS, _NC
    if _CONSTS is None:
        _CONSTS = _host_consts()
    cst = _CONSTS
    x = np.ascontiguousarray(np.asarray(L0_attn_x, dtype=np.float32))
    hperm = [8 * (g // 4) + (g % 4) + 4 * h for g in range(8) for h in range(2)]
    colperm = np.concatenate([np.arange(h * 64, (h + 1) * 64) for h in hperm])
    shared = dict(
        wq=np.ascontiguousarray(np.asarray(L0_attn_Wq, np.float32)[:, colperm]),
        wk=np.ascontiguousarray(L0_attn_Wk, np.float32),
        wv=np.ascontiguousarray(L0_attn_Wv, np.float32),
        wo=np.ascontiguousarray(np.asarray(L0_attn_Wo, np.float32)[colperm, :]),
        wrT=np.ascontiguousarray(np.asarray(L0_ffn_W_r, np.float32).T),
        wkT=np.ascontiguousarray(np.asarray(L0_ffn_W_k, np.float32).T),
        wvT=np.ascontiguousarray(np.asarray(L0_ffn_W_v, np.float32).T),
        bqT=np.ascontiguousarray(np.asarray(L0_attn_bq, np.float32)[colperm].reshape(8, 128).T),
        bkT=np.ascontiguousarray(np.asarray(L0_attn_bk, np.float32).reshape(2, 128).T),
        bv=np.ascontiguousarray(np.asarray(L0_attn_bv, np.float32).reshape(1, 256)),
        bo=np.ascontiguousarray(np.asarray(L0_attn_bo, np.float32).reshape(1, C)),
        muR=np.ascontiguousarray(np.asarray(L0_ffn_mu_r, np.float32).reshape(8, 128).T),
        muK=np.ascontiguousarray(np.asarray(L0_ffn_mu_k, np.float32).reshape(8, 128).T),
        ident=cst['ident'], tri=cst['tri'], onescol=cst['onescol'],
        pat2=cst['pat2'], pat2T=cst['pat2T'],
        ones1f=cst['ones1f'], epst=cst['epst'],
        zcol=cst['zcol'], zrow=cst['zrow'],
        cosrep=cst['cosrep'], sinrep=cst['sinrep'], permQ=cst['permQ'],
    )
    if _NC is None:
        _NC = build_program()
    in_maps = [dict(shared, x=np.ascontiguousarray(x[i])) for i in range(8)]
    ncore = int(os.environ.get("K_CORES", "8"))
    res = run_bass_kernel_spmd(_NC, in_maps[:ncore], core_ids=list(range(ncore)))
    outs = [np.asarray(r["out"], np.float32) for r in res.results]
    while len(outs) < 8:
        outs.append(np.zeros((S, C), np.float32))
    return np.stack(outs, axis=0)


# version 15
# speedup vs baseline: 1.2027x; 1.2027x over previous
import os
import numpy as np
import ml_dtypes
import concourse.bass as bass
import concourse.mybir as mybir
from concourse.bass_utils import run_bass_kernel_spmd

F32 = mybir.dt.float32
BF16 = mybir.dt.bfloat16
AF = mybir.ActivationFunctionType
ALU = mybir.AluOpType
AX = mybir.AxisListType

S = 4096
C = 1024
NQ, NKV, D = 16, 4, 64
NCH = S // 512
NT = S // 128
ROPE_BASE = 10000.0
EPS = 1e-5
BF = ml_dtypes.bfloat16

_UPTO = os.environ.get("K_UPTO", "C")
_ORDER = ["A", "A2", "B", "W", "C"]
_NCHA = int(os.environ.get("K_ACH", "8"))
_NCHC = int(os.environ.get("K_CCH", "8"))


def _host_consts():
    ident = np.eye(128, dtype=BF)
    tri = np.triu(np.ones((128, 128), np.float32)).astype(BF)
    onescol = np.ones((128, 128), dtype=BF)
    pat2 = np.zeros((2, 128), np.float32)
    pat2[0, 0:64] = 1.0
    pat2[1, 64:128] = 1.0
    ones1f = np.ones((1, 128), np.float32)
    epst = np.full((128, 1), EPS, np.float32)
    inv = 1.0 / ROPE_BASE ** (np.arange(0, D, 2, dtype=np.float64) / D)
    ang = np.arange(S, dtype=np.float64)[:, None] * inv[None, :]
    cos64 = np.concatenate([np.cos(ang), np.cos(ang)], axis=1).T
    sin64 = np.concatenate([np.sin(ang), np.sin(ang)], axis=1).T
    sinp = sin64.copy()
    sinp[0:32, :] *= -1.0
    cosrep = np.tile(cos64, (2, 1)).astype(np.float32).astype(BF)
    sinrep = np.tile(sinp, (2, 1)).astype(np.float32).astype(BF)
    permQ = np.zeros((128, 128), np.float32)
    for i in range(32):
        permQ[32 + i, i] = 1.0
        permQ[i, 32 + i] = 1.0
        permQ[96 + i, 64 + i] = 1.0
        permQ[64 + i, 96 + i] = 1.0
    return dict(ident=ident, tri=tri, onescol=onescol, pat2=pat2,
                pat2T=np.ascontiguousarray(pat2.astype(BF).T),
                ones1f=ones1f, epst=epst, cosrep=cosrep, sinrep=sinrep,
                permQ=permQ.astype(BF),
                zcol=np.zeros((128, 1), np.float32),
                zrow=np.zeros((1, 512), BF))


def build_program(upto=None):
    upto = upto or _UPTO
    lvl = _ORDER.index(upto)
    nc = bass.Bass("TRN2", target_bir_lowering=False)

    x_d = nc.dram_tensor("x", [S, C], F32, kind="ExternalInput").ap()
    out_d = nc.dram_tensor("out", [S, C], F32, kind="ExternalOutput").ap()
    wq_d = nc.dram_tensor("wq", [C, C], F32, kind="ExternalInput").ap()
    wk_d = nc.dram_tensor("wk", [C, 256], F32, kind="ExternalInput").ap()
    wv_d = nc.dram_tensor("wv", [C, 256], F32, kind="ExternalInput").ap()
    wo_d = nc.dram_tensor("wo", [C, C], F32, kind="ExternalInput").ap()
    wrT_d = nc.dram_tensor("wrT", [C, C], F32, kind="ExternalInput").ap()
    wkT_d = nc.dram_tensor("wkT", [C, C], F32, kind="ExternalInput").ap()
    wvT_d = nc.dram_tensor("wvT", [C, C], F32, kind="ExternalInput").ap()
    bqT_d = nc.dram_tensor("bqT", [128, 8], F32, kind="ExternalInput").ap()
    bkT_d = nc.dram_tensor("bkT", [128, 2], F32, kind="ExternalInput").ap()
    bv_d = nc.dram_tensor("bv", [1, 256], F32, kind="ExternalInput").ap()
    bo_d = nc.dram_tensor("bo", [1, C], F32, kind="ExternalInput").ap()
    muR_d = nc.dram_tensor("muR", [128, 8], F32, kind="ExternalInput").ap()
    muK_d = nc.dram_tensor("muK", [128, 8], F32, kind="ExternalInput").ap()
    ident_d = nc.dram_tensor("ident", [128, 128], BF16, kind="ExternalInput").ap()
    tri_d = nc.dram_tensor("tri", [128, 128], BF16, kind="ExternalInput").ap()
    onescol_d = nc.dram_tensor("onescol", [128, 128], BF16, kind="ExternalInput").ap()
    pat2_d = nc.dram_tensor("pat2", [2, 128], F32, kind="ExternalInput").ap()
    pat2T_d = nc.dram_tensor("pat2T", [128, 2], BF16, kind="ExternalInput").ap()
    ones1f_d = nc.dram_tensor("ones1f", [1, 128], F32, kind="ExternalInput").ap()
    epst_d = nc.dram_tensor("epst", [128, 1], F32, kind="ExternalInput").ap()
    zcol_d = nc.dram_tensor("zcol", [128, 1], F32, kind="ExternalInput").ap()
    zrow_d = nc.dram_tensor("zrow", [1, 512], BF16, kind="ExternalInput").ap()
    cosrep_d = nc.dram_tensor("cosrep", [128, S], BF16, kind="ExternalInput").ap()
    sinrep_d = nc.dram_tensor("sinrep", [128, S], BF16, kind="ExternalInput").ap()
    permQ_d = nc.dram_tensor("permQ", [128, 128], BF16, kind="ExternalInput").ap()
    xnT_h = nc.dram_tensor("xnT_h", [C, S], BF16).ap()
    attn_h = nc.dram_tensor("attn_h", [C, S], BF16).ap()
    den_h = nc.dram_tensor("den_h", [128, 512], F32).ap()
    rden_h = nc.dram_tensor("rden_h", [128, 512], F32).ap()

    base_ar = nc.auto_arena()
    s_ident = base_ar.sbuf_tensor("s_ident", [128, 128], BF16)
    s_tri = base_ar.sbuf_tensor("s_tri", [128, 128], BF16)
    s_ones = base_ar.sbuf_tensor("s_ones", [128, 128], BF16)
    s_pat2 = base_ar.sbuf_tensor("s_pat2", [2, 128], F32)
    s_pat2T = base_ar.sbuf_tensor("s_pat2T", [128, 2], BF16)
    s_ones1f = base_ar.sbuf_tensor("s_ones1f", [1, 128], F32)
    s_eps = base_ar.sbuf_tensor("s_eps", [128, 1], F32)
    s_zc = base_ar.sbuf_tensor("s_zc", [128, 1], F32)
    s_zrow = base_ar.sbuf_tensor("s_zrow", [1, 512], BF16)
    s_bqT = base_ar.sbuf_tensor("s_bqT", [128, 8], F32)
    s_bkT = base_ar.sbuf_tensor("s_bkT", [128, 2], F32)
    s_bv16 = base_ar.sbuf_tensor("s_bv16", [1, 256], BF16)
    s_bo = base_ar.sbuf_tensor("s_bo", [1, C], F32)
    s_muR = base_ar.sbuf_tensor("s_muR", [128, 8], F32)
    s_muK = base_ar.sbuf_tensor("s_muK", [128, 8], F32)

    qkv_ar = nc.auto_arena()
    Qbuf = [qkv_ar.sbuf_tensor(f"Qbuf{g}", [128, S], BF16) for g in range(8)]
    Kpk = [qkv_ar.sbuf_tensor(f"Kpk{h}", [128, S], BF16) for h in range(2)]
    Vbuf = qkv_ar.sbuf_tensor("Vbuf", [128, NKV * NT * 65], BF16)
    s_kabs = qkv_ar.sbuf_tensor("s_kabs", [128, 2], F32)
    s_perm = qkv_ar.sbuf_tensor("s_perm", [128, 128], BF16)

    A_ar = nc.auto_arena()
    s_xt = A_ar.sbuf_tensor("s_xt", [128, 4 * C], F32)
    s_stats = A_ar.sbuf_tensor("s_stats", [128, 12], F32)
    s_mv = A_ar.sbuf_tensor("s_mv", [128, 8], F32)
    s_lnt = A_ar.sbuf_tensor("s_lnt", [128, 4], F32)
    s_rstd = A_ar.sbuf_tensor("s_rstd", [128, 4], F32)
    s_negm = A_ar.sbuf_tensor("s_negm", [128, 4], F32)
    s_xn = A_ar.sbuf_tensor("s_xn", [128, 2 * C], BF16)
    s_xnT = A_ar.sbuf_tensor("s_xnT", [128, 2 * 8 * 512], BF16)
    s_ktmp = A_ar.sbuf_tensor("s_ktmp", [128, 1024], BF16)
    s_wq16 = A_ar.sbuf_tensor("s_wq16", [128, 8 * C], BF16)
    s_wk16 = A_ar.sbuf_tensor("s_wk16", [128, 8 * 256], BF16)
    s_wv16 = A_ar.sbuf_tensor("s_wv16", [128, 8 * 256], BF16)

    A2_ar = nc.auto_arena()
    s_cos = A2_ar.sbuf_tensor("s_cos", [128, S], BF16)
    s_sin = A2_ar.sbuf_tensor("s_sin", [128, S], BF16)
    s_rtmp = A2_ar.sbuf_tensor("s_rtmp", [128, 512], BF16)
    s_kacc = A2_ar.sbuf_tensor("s_kacc", [128, 1024], BF16)

    cw_ar = nc.auto_arena()
    s_wo16 = cw_ar.sbuf_tensor("s_wo16", [128, 8 * C], BF16)
    s_wrT = cw_ar.sbuf_tensor("s_wrT", [128, 8 * C], BF16)
    s_wkT = cw_ar.sbuf_tensor("s_wkT", [128, 8 * C], BF16)
    s_wvT = cw_ar.sbuf_tensor("s_wvT", [128, 8 * C], BF16)

    R_ar = nc.auto_arena()
    s_dsb = R_ar.sbuf_tensor("s_dsb", [128, 512], F32)
    s_rsb = R_ar.sbuf_tensor("s_rsb", [128, 512], F32)

    B_ar = nc.auto_arena()
    s_P = B_ar.sbuf_tensor("s_P", [128, 2 * 1536], BF16)
    s_qmx = B_ar.sbuf_tensor("s_qmx", [128, 8], BF16)
    s_prod = B_ar.sbuf_tensor("s_prod", [128, 8], BF16)
    s_sc = B_ar.sbuf_tensor("s_sc", [1, 64], BF16)
    s_btA = B_ar.sbuf_tensor("s_btA", [128, 64], F32)
    s_num = B_ar.sbuf_tensor("s_num", [128, 1024], BF16)
    s_den = B_ar.sbuf_tensor("s_den", [128, 1024], F32)

    C_ar = nc.auto_arena()
    s_xnTc = C_ar.sbuf_tensor("s_xnTc", [128, 2 * 8 * 513], BF16)
    s_attc = C_ar.sbuf_tensor("s_attc", [128, 2 * 8 * 512], BF16)
    s_xr = C_ar.sbuf_tensor("s_xr", [128, 2 * 8 * 512], BF16)
    s_xk = C_ar.sbuf_tensor("s_xk", [128, 2 * 8 * 512], BF16)
    s_dt = C_ar.sbuf_tensor("s_dt", [128, 512], BF16)
    s_rnat = C_ar.sbuf_tensor("s_rnat", [128, 4 * C], BF16)
    s_krelu = C_ar.sbuf_tensor("s_krelu", [128, 512], BF16)
    s_kact = C_ar.sbuf_tensor("s_kact", [128, 2 * 8 * 512], BF16)
    s_xc = C_ar.sbuf_tensor("s_xc", [128, 2 * C], F32)
    s_bob = C_ar.sbuf_tensor("s_bob", [128, C], F32)
    s_rd = C_ar.sbuf_tensor("s_rd", [2, 8 * 512], F32)
    s_f1 = C_ar.sbuf_tensor("s_f1", [128, 2 * 512], F32)
    s_out = C_ar.sbuf_tensor("s_out", [128, 2 * C], F32)

    from contextlib import ExitStack
    ctx = ExitStack()

    def sem(name):
        return ctx.enter_context(nc.semaphore(name))

    LDS = sem("LDS")
    WL = sem("WL")
    LDx = [sem(f"LDx{i}") for i in range(4)]
    LDd = sem("LDd")
    SX = [sem(f"SX{i}") for i in range(2)]
    SB = sem("SB")
    SR = sem("SR")
    LCxn = [sem(f"LCxn{i}") for i in range(2)]
    LCat = [sem(f"LCat{i}") for i in range(2)]
    LCx = [sem(f"LCx{i}") for i in range(2)]
    LCrd = [sem(f"LCrd{i}") for i in range(2)]
    SO = [sem(f"SO{i}") for i in range(2)]
    PEs = sem("PEs")
    ACs = sem("ACs")
    DVs = sem("DVs")
    n = dict(PE=0, AC=0, DV=0)

    block = ctx.enter_context(nc.Block())

    base_ar.allocate(side="left")
    qkv_ar.allocate(side="left")
    A_ar.allocate(side="left")
    A2_ar.allocate(side="right")

    NSETUP = 17

    @block.sync
    def _(e):
        for dst, src in [(s_ident, ident_d), (s_tri, tri_d), (s_ones, onescol_d),
                         (s_pat2, pat2_d), (s_pat2T, pat2T_d),
                         (s_ones1f, ones1f_d), (s_eps, epst_d),
                         (s_zc, zcol_d), (s_zrow, zrow_d), (s_bqT, bqT_d), (s_bkT, bkT_d),
                         (s_bo, bo_d), (s_muR, muR_d), (s_muK, muK_d),
                         (s_cos, cosrep_d), (s_sin, sinrep_d), (s_perm, permQ_d)]:
            e.dma_start(out=dst[:, :], in_=src).then_inc(LDS, 16)

    @block.gpsimd
    def _(e):
        e.dma_start(out=s_wq16[:, :].rearrange("p (cc d) -> p cc d", d=C),
                    in_=wq_d.rearrange("(cc p) d -> p cc d", p=128)).then_inc(WL, 16)
        e.dma_start(out=s_wk16[:, :].rearrange("p (cc d) -> p cc d", d=256),
                    in_=wk_d.rearrange("(cc p) d -> p cc d", p=128)).then_inc(WL, 16)
        e.dma_start(out=s_wv16[:, :].rearrange("p (cc d) -> p cc d", d=256),
                    in_=wv_d.rearrange("(cc p) d -> p cc d", p=128)).then_inc(WL, 16)
        e.dma_start(out=s_bv16[:, :], in_=bv_d).then_inc(WL, 16)
    WL_A = 64

    @block.vector
    def _(e):
        e.wait_ge(LDS, 16 * NSETUP)
        ap = bass.AP(Vbuf, 64, [[Vbuf.shape[1], 128], [65, NKV * NT]])
        e.tensor_copy(ap, s_ones[:, 0:NKV * NT]).then_inc(DVs, 1)
        n['DV'] += 1

    psA = ExitStack()
    ptrA = psA.enter_context(nc.psum_tensor("ptrA", [128, 512], F32))
    ptrB = psA.enter_context(nc.psum_tensor("ptrB", [128, 512], F32))
    pq = psA.enter_context(nc.psum_tensor("pq", [128, 1024], F32))
    pk = psA.enter_context(nc.psum_tensor("pk", [128, 512], F32))
    pv = psA.enter_context(nc.psum_tensor("pv", [128, 512], F32))
    prp = [psA.enter_context(nc.psum_tensor(f"prp{i}", [128, 512], F32))
           for i in range(2)]

    ACH = {}
    nA = min(NCH, _NCHA)
    for ch in range(nA):
        b = ch % 2
        pe0, ac0, dv0 = n['PE'], n['AC'], n['DV']
        stats = [dv0 + 2 * i + 1 for i in range(4)]
        negm = [dv0 + 2 * i + 2 for i in range(4)]
        vcopy = [dv0 + 8 + t + 1 for t in range(4)]
        rope_dv = [dv0 + 12 + u + 1 for u in range(10)]
        rstd_c = [ac0 + 4 * i + 1 for i in range(4)]
        xn_c = [ac0 + 4 * i + 2 for i in range(4)]
        evacA = [ac0 + 4 * i + 3 for i in range(4)]
        evacB = [ac0 + 4 * i + 4 for i in range(4)]
        qevac = [ac0 + 16 + g + 1 for g in range(8)]
        kevac = [ac0 + 24 + g2 + 1 for g2 in range(2)]
        trA = [pe0 + 2 * i + 1 for i in range(4)]
        trB = [pe0 + 2 * i + 2 for i in range(4)]
        qmm = [pe0 + 8 + g + 1 for g in range(8)]
        kmm = [pe0 + 16 + g2 + 1 for g2 in range(2)]
        vmm = [pe0 + 18 + t + 1 for t in range(4)]
        rope_pe = [pe0 + 22 + u + 1 for u in range(10)]
        ACH[ch] = dict(xn=xn_c, vmm=vmm, evacA=evacA, evacB=evacB,
                       vcopy=vcopy, qevac=qevac, kevac=kevac, trB=trB,
                       rope=rope_dv, rope_pe=rope_pe)

        @block.sync
        def _(e, ch=ch):
            for i in range(4):
                st = 4 * ch + i
                if ch >= 1:
                    e.wait_ge(ACs, ACH[ch - 1]['xn'][i])
                e.dma_start(out=s_xt[:, i * C:(i + 1) * C],
                            in_=x_d[st * 128:(st + 1) * 128, :]
                            ).then_inc(LDx[i], 16)

        @block.vector
        def _(e, ch=ch, b=b, vmm=vmm, rstd_c=rstd_c):
            for i in range(4):
                e.wait_ge(LDx[i], 16 * (ch + 1))
                xt = s_xt[:, i * C:(i + 1) * C]
                e.bn_stats(s_stats[:, 0:6], xt[:, 0:512])
                e.bn_stats(s_stats[:, 6:12], xt[:, 512:1024])
                e.drain()
                e.bn_aggr(s_mv[:, i * 2:i * 2 + 2], s_stats[:, :])
                e.drain().then_inc(DVs, 1)
                e.wait_ge(ACs, rstd_c[i])
                e.tensor_scalar(s_negm[:, i:i + 1], s_mv[:, i * 2:i * 2 + 1],
                                s_rstd[:, i:i + 1], -1.0,
                                op0=ALU.mult, op1=ALU.mult).then_inc(DVs, 1)
            for t in range(4):
                e.wait_ge(PEs, vmm[t])
                st = 4 * ch + t
                for kv in range(4):
                    ins = e.tensor_copy(
                        Vbuf[:, (kv * NT + st) * 65:(kv * NT + st) * 65 + 64],
                        pv[:, kv * 64:(kv + 1) * 64])
                ins.then_inc(DVs, 1)
            cosl = s_cos[:, ch * 512:(ch + 1) * 512]
            sinl = s_sin[:, ch * 512:(ch + 1) * 512]
            for u in range(10):
                bank = prp[u % 2]
                e.wait_ge(PEs, rope_pe[u])
                if u < 8:
                    tgt = Qbuf[u][:, ch * 512:(ch + 1) * 512]
                    e.tensor_tensor(s_rtmp[:, :], bank[:, :], sinl, op=ALU.mult)
                    e.tensor_tensor(tgt, tgt, cosl, op=ALU.mult)
                    e.tensor_tensor(tgt, tgt, s_rtmp[:, :],
                                    op=ALU.add).then_inc(DVs, 1)
                else:
                    g2 = u - 8
                    tgt = Kpk[g2][:, ch * 512:(ch + 1) * 512]
                    acc = s_kacc[:, g2 * 512:(g2 + 1) * 512]
                    e.tensor_tensor(s_rtmp[:, :], bank[:, :], sinl, op=ALU.mult)
                    e.tensor_tensor(tgt, s_ktmp[:, g2 * 512:(g2 + 1) * 512],
                                    cosl, op=ALU.mult)
                    e.tensor_tensor(tgt, tgt, s_rtmp[:, :], op=ALU.add)
                    e.tensor_scalar(s_rtmp[:, :], tgt, -1.0, None, op0=ALU.mult)
                    if ch == 0:
                        e.tensor_tensor(acc, tgt, s_rtmp[:, :],
                                        op=ALU.max).then_inc(DVs, 1)
                    else:
                        e.tensor_tensor(acc, acc, tgt, op=ALU.max)
                        e.tensor_tensor(acc, acc, s_rtmp[:, :],
                                        op=ALU.max).then_inc(DVs, 1)
        n['DV'] += 22

        @block.scalar
        def _(e, ch=ch, b=b, stats=stats, negm=negm, trA=trA, trB=trB,
              qmm=qmm, kmm=kmm):
            for i in range(4):
                st = 4 * ch + i
                sl = st % 2
                e.wait_ge(DVs, stats[i])
                e.activation(s_lnt[:, i:i + 1], s_mv[:, i * 2 + 1:i * 2 + 2],
                             AF.Ln, bias=s_eps[:, :])
                e.activation(s_rstd[:, i:i + 1], s_lnt[:, i:i + 1],
                             AF.Exp, scale=-0.5).then_inc(ACs, 1)
                e.wait_ge(DVs, negm[i])
                if st >= 2:
                    e.wait_ge(PEs, ACH[(st - 2) // 4]['trB'][(st - 2) % 4])
                e.activation(s_xn[:, sl * C:(sl + 1) * C], s_xt[:, i * C:(i + 1) * C],
                             AF.Identity, bias=s_negm[:, i:i + 1],
                             scale=s_rstd[:, i:i + 1]).then_inc(ACs, 1)
                e.wait_ge(PEs, trA[i])
                if ch >= 2 and i == 0:
                    e.wait_ge(PEs, ACH[ch - 2]['vmm'][3])
                    e.wait_ge(SX[b], 16 * 8 * (ch // 2))
                for g in range(4):
                    ins = e.activation(
                        s_xnT[:, (b * 8 + g) * 512 + i * 128:(b * 8 + g) * 512 + (i + 1) * 128],
                        ptrA[:, g * 128:(g + 1) * 128], AF.Identity)
                ins.then_inc(ACs, 1)
                e.wait_ge(PEs, trB[i])
                for g in range(4):
                    ins = e.activation(
                        s_xnT[:, (b * 8 + 4 + g) * 512 + i * 128:(b * 8 + 4 + g) * 512 + (i + 1) * 128],
                        ptrB[:, g * 128:(g + 1) * 128], AF.Identity)
                ins.then_inc(ACs, 1)
            for g in range(8):
                e.wait_ge(PEs, qmm[g])
                e.activation(Qbuf[g][:, ch * 512:(ch + 1) * 512],
                             pq[:, (g % 2) * 512:(g % 2 + 1) * 512],
                             AF.Identity, bias=s_bqT[:, g:g + 1]).then_inc(ACs, 1)
            for g2 in range(2):
                e.wait_ge(PEs, kmm[g2])
                if ch >= 1:
                    e.wait_ge(DVs, ACH[ch - 1]['rope'][8 + g2])
                e.activation(s_ktmp[:, g2 * 512:(g2 + 1) * 512], pk[:, :],
                             AF.Identity, bias=s_bkT[:, g2:g2 + 1]).then_inc(ACs, 1)
        n['AC'] += 26

        @block.gpsimd
        def _(e, ch=ch, b=b, evacB=evacB):
            e.wait_ge(ACs, evacB[3])
            for g in range(8):
                e.dma_start(out=xnT_h[g * 128:(g + 1) * 128, ch * 512:(ch + 1) * 512],
                            in_=s_xnT[:, (b * 8 + g) * 512:(b * 8 + g + 1) * 512]
                            ).then_inc(SX[b], 16)

        @block.tensor
        def _(e, ch=ch, b=b, xn_c=xn_c, evacA=evacA, evacB=evacB,
              qevac=qevac, kevac=kevac, vcopy=vcopy):
            for i in range(4):
                sl = (4 * ch + i) % 2
                e.wait_ge(ACs, xn_c[i])
                if i > 0:
                    e.wait_ge(ACs, evacA[i - 1])
                elif ch > 0:
                    e.wait_ge(ACs, ACH[ch - 1]['evacA'][3])
                for g in range(4):
                    ins = e.matmul(ptrA[:, g * 128:(g + 1) * 128],
                                   s_xn[:, sl * C + g * 128:sl * C + (g + 1) * 128],
                                   s_ident[:, :], start=(g == 0), stop=(g == 3))
                ins.then_inc(PEs, 1)
                if i > 0:
                    e.wait_ge(ACs, evacB[i - 1])
                elif ch > 0:
                    e.wait_ge(ACs, ACH[ch - 1]['evacB'][3])
                for g in range(4):
                    ins = e.matmul(ptrB[:, g * 128:(g + 1) * 128],
                                   s_xn[:, sl * C + (4 + g) * 128:sl * C + (5 + g) * 128],
                                   s_ident[:, :], start=(g == 0), stop=(g == 3))
                ins.then_inc(PEs, 1)
            e.wait_ge(ACs, evacB[3])
            if ch == 0:
                e.wait_ge(WL, WL_A)
            for g in range(8):
                if g >= 2:
                    e.wait_ge(ACs, qevac[g - 2])
                elif ch > 0:
                    e.wait_ge(ACs, ACH[ch - 1]['qevac'][6 + g])
                for cc in range(8):
                    ins = e.matmul(pq[:, (g % 2) * 512:(g % 2 + 1) * 512],
                                   s_wq16[:, cc * C + g * 128:cc * C + (g + 1) * 128],
                                   s_xnT[:, (b * 8 + cc) * 512:(b * 8 + cc + 1) * 512],
                                   start=(cc == 0), stop=(cc == 7))
                ins.then_inc(PEs, 1)
            for g2 in range(2):
                if g2 >= 1:
                    e.wait_ge(ACs, kevac[0])
                elif ch > 0:
                    e.wait_ge(ACs, ACH[ch - 1]['kevac'][1])
                for cc in range(8):
                    ins = e.matmul(pk[:, :],
                                   s_wk16[:, cc * 256 + g2 * 128:cc * 256 + (g2 + 1) * 128],
                                   s_xnT[:, (b * 8 + cc) * 512:(b * 8 + cc + 1) * 512],
                                   start=(cc == 0), stop=(cc == 7))
                ins.then_inc(PEs, 1)
            for t in range(4):
                if t >= 1:
                    e.wait_ge(DVs, vcopy[t - 1])
                elif ch >= 1:
                    e.wait_ge(DVs, ACH[ch - 1]['vcopy'][3])
                for cc in range(8):
                    e.matmul(pv[:, 0:256],
                             s_xnT[:, (b * 8 + cc) * 512 + t * 128:(b * 8 + cc) * 512 + (t + 1) * 128],
                             s_wv16[:, cc * 256:(cc + 1) * 256],
                             start=(cc == 0), stop=False)
                e.matmul(pv[:, 0:256], s_ones[0:1, 0:128], s_bv16[:, :],
                         start=False, stop=True).then_inc(PEs, 1)
            for u in range(10):
                if u >= 2:
                    e.wait_ge(DVs, rope_dv[u - 2])
                elif ch > 0:
                    e.wait_ge(DVs, ACH[ch - 1]['rope'][8 + u])
                if u < 8:
                    e.wait_ge(ACs, qevac[u])
                    src = Qbuf[u][:, ch * 512:(ch + 1) * 512]
                else:
                    e.wait_ge(ACs, kevac[u - 8])
                    src = s_ktmp[:, (u - 8) * 512:(u - 7) * 512]
                e.matmul(prp[u % 2][:, :], s_perm[:, :], src,
                         start=True, stop=True).then_inc(PEs, 1)
        n['PE'] += 32

    @block.vector
    def _(e):
        for g2 in range(2):
            e.reduce_max(s_kabs[:, g2:g2 + 1],
                         s_kacc[:, g2 * 512:(g2 + 1) * 512], axis=AX.X)
        e.drain()
        e.tensor_scalar(s_kabs[:, 0:2], s_kabs[:, 0:2], 0.125, None,
                        op0=ALU.mult).then_inc(DVs, 1)
    n['DV'] += 1
    A_done = dict(n)

    store_state = dict(SXa=16 * 8 * ((nA + 1) // 2), SXb=16 * 8 * (nA // 2),
                       SB=0, SR=0, SOa=0, SOb=0)

    def store_totals():
        return [(SX[0], store_state['SXa']), (SX[1], store_state['SXb']),
                (SB, store_state['SB']), (SR, store_state['SR']),
                (SO[0], store_state['SOa']), (SO[1], store_state['SOb'])]

    def _finish(open_psums, open_left, open_right):
        @block.scalar
        def _(e):
            for s, tot in store_totals():
                if tot:
                    e.wait_ge(s, tot)
        for ps in open_psums:
            ps.close()
        for ar in open_left:
            ar.free()
        for ar in open_right:
            ar.free()
        base_ar.free()
        ctx.close()
        return nc

    if lvl == 0:
        return _finish([psA], [A_ar, qkv_ar], [A2_ar])

    A2_dv = n['DV']

    psA.close()
    A_ar.free()
    A2_ar.free()
    cw_ar.allocate(side="right")
    B_ar.allocate(side="left")

    if lvl == 1:
        return _finish([], [B_ar, qkv_ar], [cw_ar])

    @block.gpsimd
    def _(e):
        e.wait_ge(DVs, A2_dv)
        e.dma_start(out=s_wo16[:, :].rearrange("p (g d) -> p g d", d=C),
                    in_=wo_d.rearrange("(g p) d -> p g d", p=128)).then_inc(WL, 16)
        for dst, src in [(s_wrT, wrT_d), (s_wkT, wkT_d), (s_wvT, wvT_d)]:
            e.dma_start(out=dst[:, :].rearrange("p (cc d) -> p cc d", d=C),
                        in_=src.rearrange("(cc p) d -> p cc d", p=128)).then_inc(WL, 16)
    WL_ALL = WL_A + 64

    psB = ExitStack()
    pS = psB.enter_context(nc.psum_tensor("pS", [128, 3072], F32))
    pAV = psB.enter_context(nc.psum_tensor("pAV", [128, 1024], F32))

    @block.tensor
    def _(e):
        e.wait_ge(LDS, 16 * NSETUP)
        e.wait_ge(PEs, A_done['PE'])
        for k2 in range(6):
            ins = e.matmul(pS[:, k2 * 512:(k2 + 1) * 512], s_ones[0:1, :],
                           s_zrow[:, :], start=True, stop=True)
        ins.then_inc(PEs, 1)
    n['PE'] += 1

    bq_dv0 = n['DV']
    bq_pe0 = n['PE']

    @block.vector
    def _(e):
        e.wait_ge(PEs, A_done['PE'])
        e.wait_ge(ACs, A_done['AC'])
        for pr in range(8):
            cpx = pr // 4
            for qt in range(NCH):
                e.reduce_max(s_qmx[:, qt:qt + 1],
                             Qbuf[pr][:, qt * 512:(qt + 1) * 512], axis=AX.X,
                             apply_absolute_value=True)
            e.drain()
            ins = None
            for qt in range(NCH):
                ins = e.tensor_scalar(s_prod[:, qt:qt + 1], s_qmx[:, qt:qt + 1],
                                      s_kabs[:, cpx:cpx + 1], -1.0,
                                      op0=ALU.mult, op1=ALU.mult)
            ins.then_inc(DVs, 1)
            e.wait_ge(PEs, bq_pe0 + pr + 1)
            e.tensor_copy(s_sc[0:1, pr * 8:(pr + 1) * 8],
                          pAV[0:1, 0:8]).then_inc(DVs, 1)
        e.wait_ge(PEs, bq_pe0 + 9)
        e.tensor_copy(s_btA[:, 0:64], pAV[:, 512:576]).then_inc(DVs, 1)
    n['DV'] += 17
    dv_btA = n['DV']

    @block.tensor
    def _(e):
        for pr in range(8):
            e.wait_ge(DVs, bq_dv0 + 2 * pr + 1)
            if pr > 0:
                e.wait_ge(DVs, bq_dv0 + 2 * pr)
            e.matmul(pAV[0:1, 0:8], s_pat2T[:, 0:1], s_prod[:, 0:8],
                     start=True, stop=True).then_inc(PEs, 1)
        e.wait_ge(DVs, bq_dv0 + 16)
        e.matmul(pAV[:, 512:576], s_ones[0:1, :], s_sc[0:1, 0:64],
                 start=True, stop=True).then_inc(PEs, 1)
    n['PE'] += 9

    last_exp = 0
    avevac_prev = 0
    pair_serial = 0

    for qt in range(NCH):
        for pr in range(8):
            cpr = pr // 4
            nfull = 4 * qt
            units = [('f', t, hh) for t in range(nfull) for hh in range(2)] \
                  + [('d', kk, hh) for kk in range(4) for hh in range(2)]
            nu = len(units)
            batches = [units[i:i + 3] for i in range(0, nu, 3)]
            nb = len(batches)
            pe0, ac0, dv0 = n['PE'], n['AC'], n['DV']

            pe_scores = [pe0 + 1]
            for i in range(1, nb):
                pe_scores.append(pe0 + 2 * i)
            pe_avs = [pe0 + 2 * i + 3 for i in range(nb - 1)]
            pe_avs.append(pe0 + 2 * nb)
            ac_exp = [ac0 + 1 + i for i in range(nb)]
            dv_masks = {}
            dvc = dv0
            for i, bt in enumerate(batches):
                if any(u[0] == 'd' for u in bt):
                    dvc += 1
                    dv_masks[i] = dvc
            dv_avevac = dvc + 1
            par = pair_serial % 2

            @block.vector
            def _(e, qt=qt, pr=pr, batches=batches, pe_avs=pe_avs,
                  ac_exp=ac_exp, dv_masks=dv_masks, pair_serial=pair_serial):
                for i, bt in enumerate(batches):
                    if not any(u[0] == 'd' for u in bt):
                        continue
                    e.wait_ge(ACs, ac_exp[i])
                    ins = None
                    for pos, u in enumerate(bt):
                        if u[0] != 'd':
                            continue
                        kk = u[1]
                        base = (i % 2) * 1536 + pos * 512 + kk * 128
                        ins = e.tensor_tensor(s_P[:, base:base + 128],
                                              s_P[:, base:base + 128],
                                              s_tri[:, :], op=ALU.mult)
                    ins.then_inc(DVs, 1)
                e.wait_ge(PEs, pe_avs[-1])
                if pair_serial >= 1:
                    e.wait_ge(SB, 64 * pair_serial)
                e.tensor_copy(s_num[0:64, 0:512], pAV[0:64, 0:512])
                e.tensor_copy(s_num[0:64, 512:1024], pAV[0:64, 512:1024])
                e.tensor_copy(s_den[64:65, 0:512], pAV[64:65, 0:512])
                ins = e.tensor_copy(s_den[64:65, 512:1024], pAV[64:65, 512:1024])
                ins.then_inc(DVs, 1)
            n['DV'] = dv_avevac

            @block.scalar
            def _(e, qt=qt, pr=pr, batches=batches,
                  pe_scores=pe_scores):
                for i, bt in enumerate(batches):
                    e.wait_ge(PEs, pe_scores[i])
                    if i == 0 and qt == 0:
                        e.wait_ge(DVs, dv_btA)
                    w = len(bt) * 512
                    e.activation(s_P[:, (i % 2) * 1536:(i % 2) * 1536 + w],
                                 pS[:, (i % 2) * 1536:(i % 2) * 1536 + w],
                                 AF.Exp, scale=0.125,
                                 bias=s_btA[:, pr * 8 + qt:pr * 8 + qt + 1]
                                 ).then_inc(ACs, 1)
            n['AC'] = ac_exp[-1]

            @block.gpsimd
            def _(e, qt=qt, pr=pr, dv_avevac=dv_avevac):
                e.wait_ge(DVs, dv_avevac)
                he, ho = 2 * pr, 2 * pr + 1
                e.dma_start(out=attn_h[he * 64:(he + 1) * 64, qt * 512:(qt + 1) * 512],
                            in_=s_num[0:64, 0:512]).then_inc(SB, 16)
                e.dma_start(out=attn_h[ho * 64:(ho + 1) * 64, qt * 512:(qt + 1) * 512],
                            in_=s_num[0:64, 512:1024]).then_inc(SB, 16)
                e.dma_start(out=den_h[qt * 16 + he:qt * 16 + he + 1, :],
                            in_=s_den[64:65, 0:512]).then_inc(SB, 16)
                e.dma_start(out=den_h[qt * 16 + ho:qt * 16 + ho + 1, :],
                            in_=s_den[64:65, 512:1024]).then_inc(SB, 16)

            @block.tensor
            def _(e, qt=qt, pr=pr, cpr=cpr, batches=batches,
                  last_exp=last_exp, avevac_prev=avevac_prev,
                  dv_masks=dv_masks, ac_exp=ac_exp, nb=nb,
                  pair_serial=pair_serial):
                e.wait_ge(ACs, last_exp)
                started = {0: False, 1: False}

                def emit_scores(i, bt):
                    if i >= 2:
                        e.wait_ge(ACs, ac_exp[i - 2])
                    ins = None
                    for pos, u in enumerate(bt):
                        hh = u[2]
                        base = 64 * hh
                        slot = (i % 2) * 1536 + pos * 512
                        if u[0] == 'f':
                            t = u[1]
                            ins = e.matmul(pS[:, slot:slot + 512],
                                           Kpk[cpr][base:base + 64, t * 128:(t + 1) * 128],
                                           Qbuf[pr][base:base + 64, qt * 512:(qt + 1) * 512],
                                           start=True, stop=True, tile_position=(base, 0))
                        else:
                            kk = u[1]
                            c0 = qt * 512 + kk * 128
                            ins = e.matmul(pS[:, slot + kk * 128:slot + 512],
                                           Kpk[cpr][base:base + 64, c0:c0 + 128],
                                           Qbuf[pr][base:base + 64, c0:(qt + 1) * 512],
                                           start=True, stop=True, tile_position=(base, 0))
                    ins.then_inc(PEs, 1)

                def emit_avs(i, bt):
                    if i == 0:
                        e.wait_ge(DVs, dv_btA if pair_serial == 0 else avevac_prev)
                    e.wait_ge(ACs, ac_exp[i])
                    if i in dv_masks:
                        e.wait_ge(DVs, dv_masks[i])
                    ins = None
                    for pos, u in enumerate(bt):
                        hh = u[2]
                        t = u[1] if u[0] == 'f' else qt * 4 + u[1]
                        slot = (i % 2) * 1536 + pos * 512
                        off = 0 if u[0] == 'f' else u[1] * 128
                        is_last = (u[0] == 'd' and u[1] == 3)
                        ins = e.matmul(pAV[0:65, hh * 512 + off:hh * 512 + 512],
                                       Vbuf[:, ((2 * cpr + hh) * NT + t) * 65:((2 * cpr + hh) * NT + t) * 65 + 65],
                                       s_P[:, slot + off:slot + 512],
                                       start=(not started[hh]), stop=is_last)
                        started[hh] = True
                    ins.then_inc(PEs, 1)

                emit_scores(0, batches[0])
                for i in range(1, nb):
                    emit_scores(i, batches[i])
                    emit_avs(i - 1, batches[i - 1])
                emit_avs(nb - 1, batches[nb - 1])
            n['PE'] = pe_avs[-1]
            last_exp = ac_exp[-1]
            avevac_prev = dv_avevac
            pair_serial += 1

    B_done = dict(n)
    store_state['SB'] = 64 * 64

    if lvl == 2:
        return _finish([psB], [B_ar, qkv_ar], [cw_ar])

    R_ar.allocate(side="right")

    @block.sync
    def _(e):
        e.wait_ge(SB, store_state['SB'])
        e.dma_start(out=s_dsb[:, :], in_=den_h).then_inc(LDd, 16)

    @block.vector
    def _(e):
        e.wait_ge(LDd, 16)
        e.reciprocal(s_rsb[:, :], s_dsb[:, :]).then_inc(DVs, 1)
    n['DV'] += 1
    dv_recip = n['DV']

    @block.gpsimd
    def _(e):
        e.wait_ge(DVs, dv_recip)
        e.dma_start(out=rden_h, in_=s_rsb[:, :]).then_inc(SR, 16)
    store_state['SR'] = 16

    psB.close()
    W_done = dict(n)

    if lvl == 3:
        return _finish([], [B_ar, qkv_ar], [R_ar, cw_ar])

    B_ar.free()
    qkv_ar.free()
    R_ar.free()
    C_ar.allocate(side="left")

    psC = ExitStack()
    pbc = psC.enter_context(nc.psum_tensor("pbc", [128, 512], F32))
    pkC = psC.enter_context(nc.psum_tensor("pkC", [128, 512], F32))
    prC = psC.enter_context(nc.psum_tensor("prC", [128, 1024], F32))
    pvC = psC.enter_context(nc.psum_tensor("pvC", [128, 1024], F32))
    pwC = psC.enter_context(nc.psum_tensor("pwC", [128, 1024], F32))

    @block.tensor
    def _(e):
        e.wait_ge(LDS, 16 * NSETUP)
        e.wait_ge(DVs, W_done['DV'])
        e.wait_ge(ACs, B_done['AC'])
        e.matmul(pwC[:, 0:512], s_ones1f[:, :], s_bo[:, 0:512], start=True, stop=True)
        e.matmul(pwC[:, 512:1024], s_ones1f[:, :], s_bo[:, 512:1024],
                 start=True, stop=True).then_inc(PEs, 1)
    n['PE'] += 1
    pe_bob = n['PE']

    @block.vector
    def _(e):
        e.wait_ge(PEs, pe_bob)
        e.tensor_copy(s_bob[:, :], pwC[:, :]).then_inc(DVs, 1)
    n['DV'] += 1
    dv_bob = n['DV']

    CCH = {}
    nC = min(NCH, _NCHC)
    for ch in range(nC):
        b = ch % 2
        pe0, ac0, dv0 = n['PE'], n['AC'], n['DV']
        dvc = dv0
        if ch == 0:
            dvc += 1
            mix_ct = []
            for g in range(8):
                dvc += 1
                mix_ct.append(dvc)
        else:
            mix_ct = CCH[ch - 1]['mixT']
        attnN_ct = []
        for g in range(8):
            dvc += 1
            attnN_ct.append(dvc)
        kact_ct = []
        for oc in range(8):
            dvc += 1
            kact_ct.append(dvc)
        xplus_ct = []
        final_ct = {}
        for t in range(4):
            dvc += 1
            xplus_ct.append(dvc)
            for dh in range(2):
                final_ct[(t, dh)] = dvc + 3
                dvc += 3
        mixT_ct = []
        if ch < min(NCH, _NCHC) - 1:
            for g in range(8):
                dvc += 1
                mixT_ct.append(dvc)
        pec = pe0
        bc_ct = []
        for g in range(8):
            pec += 1
            bc_ct.append(pec)
        kmm_ct = []
        for oc in range(8):
            pec += 1
            kmm_ct.append(pec)
        rmm_ct = {}
        vmm_ct = {}
        womm_ct = {}
        for t in range(4):
            for dh in range(2):
                pec += 1
                rmm_ct[(t, dh)] = pec
            for dh in range(2):
                pec += 1
                vmm_ct[(t, dh)] = pec
            for dh in range(2):
                pec += 1
                womm_ct[(t, dh)] = pec
        acc = ac0
        krelu_ct = []
        for oc in range(8):
            acc += 1
            krelu_ct.append(acc)
        sig_ct = {}
        for t in range(4):
            for dh in range(2):
                acc += 1
                sig_ct[(t, dh)] = acc
        CCH[ch] = dict(final=final_ct, mix=mix_ct, mixT=mixT_ct, attnN=attnN_ct,
                       kact=kact_ct, xplus=xplus_ct, sig=sig_ct, bc=bc_ct,
                       krelu=krelu_ct, last_pe=pec, last_dv=dvc)

        @block.sync
        def _(e, ch=ch, b=b):
            if ch == 0:
                e.wait_ge(PEs, W_done['PE'])
                e.wait_ge(DVs, W_done['DV'])
                e.wait_ge(ACs, B_done['AC'])
                e.wait_ge(SR, 16)
            for g in range(8):
                if ch >= 1:
                    e.wait_ge(PEs, CCH[ch - 1]['bc'][g])
                e.dma_start(out=s_rd[0:2, g * 512:(g + 1) * 512],
                            in_=rden_h[ch * 16 + 2 * g:ch * 16 + 2 * g + 2, :]
                            ).then_inc(LCrd[g % 2], 16)
            if ch >= 2:
                e.wait_ge(PEs, CCH[ch - 2]['last_pe'])
                e.wait_ge(DVs, CCH[ch - 2]['last_dv'])
            for g in range(8):
                if ch == 0:
                    dst = s_xnTc[:, (b * 8 + g) * 513 + 1:(b * 8 + g) * 513 + 513]
                    src = xnT_h[g * 128:(g + 1) * 128, 0:512]
                else:
                    dst = s_xnTc[:, (b * 8 + g) * 513:(b * 8 + g + 1) * 513]
                    src = xnT_h[g * 128:(g + 1) * 128, ch * 512 - 1:(ch + 1) * 512]
                e.dma_start(out=dst, in_=src).then_inc(LCxn[b], 16)
            for g in range(8):
                e.dma_start(out=s_attc[:, (b * 8 + g) * 512:(b * 8 + g + 1) * 512],
                            in_=attn_h[g * 128:(g + 1) * 128, ch * 512:(ch + 1) * 512]
                            ).then_inc(LCat[b], 16)

        @block.gpsimd
        def _(e, ch=ch, final_ct=final_ct):
            for t in range(4):
                tg = ch * 4 + t
                if tg >= 2:
                    prev_ch, prev_t = (tg - 2) // 4, (tg - 2) % 4
                    e.wait_ge(DVs, CCH[prev_ch]['final'][(prev_t, 1)])
                e.dma_start(out=s_xc[:, (tg % 2) * C:(tg % 2 + 1) * C],
                            in_=x_d[tg * 128:(tg + 1) * 128, :]
                            ).then_inc(LCx[tg % 2], 16)
            for t in range(4):
                tg = ch * 4 + t
                e.wait_ge(DVs, final_ct[(t, 1)])
                e.dma_start(out=out_d[tg * 128:(tg + 1) * 128, :],
                            in_=s_out[:, (tg % 2) * C:(tg % 2 + 1) * C]
                            ).then_inc(SO[tg % 2], 16)

        def emit_mix(e, tch):
            bt = tch % 2
            e.wait_ge(LCxn[bt], 16 * 8 * (tch // 2 + 1))
            if tch == 0:
                ins = None
                for g in range(8):
                    ap = s_xnTc[:, (bt * 8 + g) * 513:(bt * 8 + g) * 513 + 1]
                    ins = e.tensor_copy(ap, s_zc[:, :])
                ins.then_inc(DVs, 1)
            if tch >= 2:
                e.wait_ge(PEs, CCH[tch - 2]['last_pe'])
            for g in range(8):
                cur = s_xnTc[:, (bt * 8 + g) * 513 + 1:(bt * 8 + g) * 513 + 513]
                prev = s_xnTc[:, (bt * 8 + g) * 513:(bt * 8 + g) * 513 + 512]
                e.tensor_tensor(s_dt[:, :], cur, prev, op=ALU.subtract)
                e.scalar_tensor_tensor(s_xr[:, (bt * 8 + g) * 512:(bt * 8 + g + 1) * 512],
                                       s_dt[:, :], s_muR[:, g:g + 1], prev,
                                       op0=ALU.mult, op1=ALU.add)
                e.scalar_tensor_tensor(s_xk[:, (bt * 8 + g) * 512:(bt * 8 + g + 1) * 512],
                                       s_dt[:, :], s_muK[:, g:g + 1], prev,
                                       op0=ALU.mult, op1=ALU.add).then_inc(DVs, 1)

        @block.vector
        def _(e, ch=ch, b=b, bc_ct=bc_ct, vmm_ct=vmm_ct, womm_ct=womm_ct,
              krelu_ct=krelu_ct, sig_ct=sig_ct, emit_mix=emit_mix):
            if ch == 0:
                emit_mix(e, 0)
            e.wait_ge(LCat[b], 16 * 8 * (ch // 2 + 1))
            for g in range(8):
                e.wait_ge(PEs, bc_ct[g])
                e.tensor_tensor(s_attc[:, (b * 8 + g) * 512:(b * 8 + g + 1) * 512],
                                s_attc[:, (b * 8 + g) * 512:(b * 8 + g + 1) * 512],
                                pbc[:, :], op=ALU.mult).then_inc(DVs, 1)
            for oc in range(8):
                e.wait_ge(ACs, krelu_ct[oc])
                e.tensor_tensor(s_kact[:, (b * 8 + oc) * 512:(b * 8 + oc + 1) * 512],
                                s_krelu[:, :], s_krelu[:, :],
                                op=ALU.mult).then_inc(DVs, 1)
            for t in range(4):
                tg = ch * 4 + t
                e.wait_ge(LCx[tg % 2], 16 * (tg // 2 + 1))
                if ch == 0 and t == 0:
                    e.wait_ge(DVs, dv_bob)
                e.tensor_tensor(s_xc[:, (tg % 2) * C:(tg % 2 + 1) * C],
                                s_xc[:, (tg % 2) * C:(tg % 2 + 1) * C],
                                s_bob[:, :], op=ALU.add).then_inc(DVs, 1)
                for dh in range(2):
                    e.wait_ge(PEs, vmm_ct[(t, dh)])
                    e.wait_ge(ACs, sig_ct[(t, dh)])
                    e.tensor_tensor(s_f1[:, dh * 512:(dh + 1) * 512],
                                    s_rnat[:, t * C + dh * 512:t * C + (dh + 1) * 512],
                                    pvC[:, dh * 512:(dh + 1) * 512],
                                    op=ALU.mult).then_inc(DVs, 1)
                    e.wait_ge(PEs, womm_ct[(t, dh)])
                    e.tensor_tensor(s_f1[:, dh * 512:(dh + 1) * 512],
                                    s_f1[:, dh * 512:(dh + 1) * 512],
                                    pwC[:, dh * 512:(dh + 1) * 512],
                                    op=ALU.add).then_inc(DVs, 1)
                    if tg >= 2 and dh == 0:
                        e.wait_ge(SO[tg % 2], 16 * (tg // 2))
                    e.tensor_tensor(s_out[:, (tg % 2) * C + dh * 512:(tg % 2) * C + (dh + 1) * 512],
                                    s_f1[:, dh * 512:(dh + 1) * 512],
                                    s_xc[:, (tg % 2) * C + dh * 512:(tg % 2) * C + (dh + 1) * 512],
                                    op=ALU.add).then_inc(DVs, 1)
            if ch < nC - 1:
                emit_mix(e, ch + 1)
        n['DV'] = dvc

        @block.scalar
        def _(e, ch=ch, b=b, kmm_ct=kmm_ct, rmm_ct=rmm_ct, final_ct=final_ct):
            for oc in range(8):
                e.wait_ge(PEs, kmm_ct[oc])
                if oc >= 1:
                    e.wait_ge(DVs, CCH[ch]['kact'][oc - 1])
                elif ch >= 1:
                    e.wait_ge(DVs, CCH[ch - 1]['kact'][7])
                e.activation(s_krelu[:, :], pkC[:, :], AF.Relu).then_inc(ACs, 1)
            for t in range(4):
                for dh in range(2):
                    e.wait_ge(PEs, rmm_ct[(t, dh)])
                    if ch >= 1:
                        e.wait_ge(DVs, CCH[ch - 1]['final'][(t, dh)] - 2)
                    e.activation(s_rnat[:, t * C + dh * 512:t * C + (dh + 1) * 512],
                                 prC[:, dh * 512:(dh + 1) * 512],
                                 AF.Sigmoid).then_inc(ACs, 1)
        n['AC'] = acc

        @block.tensor
        def _(e, ch=ch, b=b, mix_ct=mix_ct, attnN_ct=attnN_ct,
              kact_ct=kact_ct, krelu_ct=krelu_ct, sig_ct=sig_ct,
              final_ct=final_ct):
            if ch == 0:
                e.wait_ge(WL, WL_ALL)
                e.wait_ge(DVs, W_done['DV'])
            for g in range(8):
                if g == 0:
                    e.wait_ge(LCrd[0], 16 * 4 * (ch + 1))
                    e.wait_ge(LCrd[1], 16 * 4 * (ch + 1))
                if g >= 1:
                    e.wait_ge(DVs, attnN_ct[g - 1])
                elif ch >= 1:
                    e.wait_ge(DVs, CCH[ch - 1]['attnN'][7])
                e.matmul(pbc[:, :], s_pat2[:, :],
                         s_rd[0:2, g * 512:(g + 1) * 512],
                         start=True, stop=True).then_inc(PEs, 1)
            e.wait_ge(DVs, mix_ct[7])
            for oc in range(8):
                if oc >= 1:
                    e.wait_ge(ACs, krelu_ct[oc - 1])
                elif ch >= 1:
                    e.wait_ge(ACs, CCH[ch - 1]['krelu'][7])
                for cc in range(8):
                    ins = e.matmul(pkC[:, :],
                                   s_wkT[:, cc * C + oc * 128:cc * C + (oc + 1) * 128],
                                   s_xk[:, (b * 8 + cc) * 512:(b * 8 + cc + 1) * 512],
                                   start=(cc == 0), stop=(cc == 7))
                ins.then_inc(PEs, 1)
            for t in range(4):
                for dh in range(2):
                    if t > 0:
                        e.wait_ge(ACs, sig_ct[(t - 1, dh)])
                    elif ch > 0:
                        e.wait_ge(ACs, CCH[ch - 1]['sig'][(3, dh)])
                    for cc in range(8):
                        ins = e.matmul(prC[:, dh * 512:(dh + 1) * 512],
                                       s_xr[:, (b * 8 + cc) * 512 + t * 128:(b * 8 + cc) * 512 + (t + 1) * 128],
                                       s_wrT[:, cc * C + dh * 512:cc * C + (dh + 1) * 512],
                                       start=(cc == 0), stop=(cc == 7))
                    ins.then_inc(PEs, 1)
                for dh in range(2):
                    e.wait_ge(DVs, kact_ct[7])
                    if t > 0:
                        e.wait_ge(DVs, final_ct[(t - 1, dh)] - 2)
                    elif ch > 0:
                        e.wait_ge(DVs, CCH[ch - 1]['final'][(3, dh)] - 2)
                    for oc in range(8):
                        ins = e.matmul(pvC[:, dh * 512:(dh + 1) * 512],
                                       s_kact[:, (b * 8 + oc) * 512 + t * 128:(b * 8 + oc) * 512 + (t + 1) * 128],
                                       s_wvT[:, oc * C + dh * 512:oc * C + (dh + 1) * 512],
                                       start=(oc == 0), stop=(oc == 7))
                    ins.then_inc(PEs, 1)
                for dh in range(2):
                    e.wait_ge(DVs, attnN_ct[7])
                    if t > 0:
                        e.wait_ge(DVs, final_ct[(t - 1, dh)] - 1)
                    elif ch > 0:
                        e.wait_ge(DVs, CCH[ch - 1]['final'][(3, dh)] - 1)
                    for g in range(8):
                        ins = e.matmul(pwC[:, dh * 512:(dh + 1) * 512],
                                       s_attc[:, (b * 8 + g) * 512 + t * 128:(b * 8 + g) * 512 + (t + 1) * 128],
                                       s_wo16[:, g * C + dh * 512:g * C + (dh + 1) * 512],
                                       start=(g == 0), stop=(g == 7))
                    ins.then_inc(PEs, 1)
        n['PE'] = pec

    store_state['SOa'] = 16 * ((4 * nC + 1) // 2)
    store_state['SOb'] = 16 * ((4 * nC) // 2)

    return _finish([psC], [C_ar], [cw_ar])


_CONSTS = None
_NC = None


def kernel(L0_attn_x, L0_attn_Wq, L0_attn_bq, L0_attn_Wk, L0_attn_bk,
           L0_attn_Wv, L0_attn_bv, L0_attn_Wo, L0_attn_bo,
           L0_ffn_W_r, L0_ffn_W_k, L0_ffn_W_v, L0_ffn_mu_r, L0_ffn_mu_k):
    global _CONSTS, _NC
    if _CONSTS is None:
        _CONSTS = _host_consts()
    cst = _CONSTS
    x = np.ascontiguousarray(np.asarray(L0_attn_x, dtype=np.float32))
    hperm = [8 * (g // 4) + (g % 4) + 4 * h for g in range(8) for h in range(2)]
    colperm = np.concatenate([np.arange(h * 64, (h + 1) * 64) for h in hperm])
    shared = dict(
        wq=np.ascontiguousarray(np.asarray(L0_attn_Wq, np.float32)[:, colperm]),
        wk=np.ascontiguousarray(L0_attn_Wk, np.float32),
        wv=np.ascontiguousarray(L0_attn_Wv, np.float32),
        wo=np.ascontiguousarray(np.asarray(L0_attn_Wo, np.float32)[colperm, :]),
        wrT=np.ascontiguousarray(np.asarray(L0_ffn_W_r, np.float32).T),
        wkT=np.ascontiguousarray(np.asarray(L0_ffn_W_k, np.float32).T),
        wvT=np.ascontiguousarray(np.asarray(L0_ffn_W_v, np.float32).T),
        bqT=np.ascontiguousarray(np.asarray(L0_attn_bq, np.float32)[colperm].reshape(8, 128).T),
        bkT=np.ascontiguousarray(np.asarray(L0_attn_bk, np.float32).reshape(2, 128).T),
        bv=np.ascontiguousarray(np.asarray(L0_attn_bv, np.float32).reshape(1, 256)),
        bo=np.ascontiguousarray(np.asarray(L0_attn_bo, np.float32).reshape(1, C)),
        muR=np.ascontiguousarray(np.asarray(L0_ffn_mu_r, np.float32).reshape(8, 128).T),
        muK=np.ascontiguousarray(np.asarray(L0_ffn_mu_k, np.float32).reshape(8, 128).T),
        ident=cst['ident'], tri=cst['tri'], onescol=cst['onescol'],
        pat2=cst['pat2'], pat2T=cst['pat2T'],
        ones1f=cst['ones1f'], epst=cst['epst'],
        zcol=cst['zcol'], zrow=cst['zrow'],
        cosrep=cst['cosrep'], sinrep=cst['sinrep'], permQ=cst['permQ'],
    )
    if _NC is None:
        _NC = build_program()
    in_maps = [dict(shared, x=np.ascontiguousarray(x[i])) for i in range(8)]
    ncore = int(os.environ.get("K_CORES", "8"))
    res = run_bass_kernel_spmd(_NC, in_maps[:ncore], core_ids=list(range(ncore)))
    outs = [np.asarray(r["out"], np.float32) for r in res.results]
    while len(outs) < 8:
        outs.append(np.zeros((S, C), np.float32))
    return np.stack(outs, axis=0)


# version 16 (final)
# speedup vs baseline: 1.2045x; 1.0015x over previous
import os
import numpy as np
import ml_dtypes
import concourse.bass as bass
import concourse.mybir as mybir
from concourse.bass_utils import run_bass_kernel_spmd

F32 = mybir.dt.float32
BF16 = mybir.dt.bfloat16
AF = mybir.ActivationFunctionType
ALU = mybir.AluOpType
AX = mybir.AxisListType

S = 4096
C = 1024
NQ, NKV, D = 16, 4, 64
NCH = S // 512
NT = S // 128
ROPE_BASE = 10000.0
EPS = 1e-5
BF = ml_dtypes.bfloat16

_UPTO = os.environ.get("K_UPTO", "C")
_ORDER = ["A", "A2", "B", "W", "C"]
_NCHA = int(os.environ.get("K_ACH", "8"))
_NCHC = int(os.environ.get("K_CCH", "8"))


def _host_consts():
    ident = np.eye(128, dtype=BF)
    tri = np.triu(np.ones((128, 128), np.float32)).astype(BF)
    onescol = np.ones((128, 128), dtype=BF)
    pat2 = np.zeros((2, 128), np.float32)
    pat2[0, 0:64] = 1.0
    pat2[1, 64:128] = 1.0
    ones1f = np.ones((1, 128), np.float32)
    epst = np.full((128, 1), EPS, np.float32)
    inv = 1.0 / ROPE_BASE ** (np.arange(0, D, 2, dtype=np.float64) / D)
    ang = np.arange(S, dtype=np.float64)[:, None] * inv[None, :]
    cos64 = np.concatenate([np.cos(ang), np.cos(ang)], axis=1).T
    sin64 = np.concatenate([np.sin(ang), np.sin(ang)], axis=1).T
    sinp = sin64.copy()
    sinp[0:32, :] *= -1.0
    cosrep = np.tile(cos64, (2, 1)).astype(np.float32).astype(BF)
    sinrep = np.tile(sinp, (2, 1)).astype(np.float32).astype(BF)
    permQ = np.zeros((128, 128), np.float32)
    for i in range(32):
        permQ[32 + i, i] = 1.0
        permQ[i, 32 + i] = 1.0
        permQ[96 + i, 64 + i] = 1.0
        permQ[64 + i, 96 + i] = 1.0
    return dict(ident=ident, tri=tri, onescol=onescol, pat2=pat2,
                pat2T=np.ascontiguousarray(pat2.astype(BF).T),
                ones1f=ones1f, epst=epst, cosrep=cosrep, sinrep=sinrep,
                permQ=permQ.astype(BF),
                zcol=np.zeros((128, 1), np.float32),
                zrow=np.zeros((1, 512), BF))


def build_program(upto=None):
    upto = upto or _UPTO
    lvl = _ORDER.index(upto)
    nc = bass.Bass("TRN2", target_bir_lowering=False)

    x_d = nc.dram_tensor("x", [S, C], F32, kind="ExternalInput").ap()
    out_d = nc.dram_tensor("out", [S, C], F32, kind="ExternalOutput").ap()
    wq_d = nc.dram_tensor("wq", [C, C], F32, kind="ExternalInput").ap()
    wk_d = nc.dram_tensor("wk", [C, 256], F32, kind="ExternalInput").ap()
    wv_d = nc.dram_tensor("wv", [C, 256], F32, kind="ExternalInput").ap()
    wo_d = nc.dram_tensor("wo", [C, C], F32, kind="ExternalInput").ap()
    wrT_d = nc.dram_tensor("wrT", [C, C], F32, kind="ExternalInput").ap()
    wkT_d = nc.dram_tensor("wkT", [C, C], F32, kind="ExternalInput").ap()
    wvT_d = nc.dram_tensor("wvT", [C, C], F32, kind="ExternalInput").ap()
    bqT_d = nc.dram_tensor("bqT", [128, 8], F32, kind="ExternalInput").ap()
    bkT_d = nc.dram_tensor("bkT", [128, 2], F32, kind="ExternalInput").ap()
    bv_d = nc.dram_tensor("bv", [1, 256], F32, kind="ExternalInput").ap()
    bo_d = nc.dram_tensor("bo", [1, C], F32, kind="ExternalInput").ap()
    muR_d = nc.dram_tensor("muR", [128, 8], F32, kind="ExternalInput").ap()
    muK_d = nc.dram_tensor("muK", [128, 8], F32, kind="ExternalInput").ap()
    ident_d = nc.dram_tensor("ident", [128, 128], BF16, kind="ExternalInput").ap()
    tri_d = nc.dram_tensor("tri", [128, 128], BF16, kind="ExternalInput").ap()
    onescol_d = nc.dram_tensor("onescol", [128, 128], BF16, kind="ExternalInput").ap()
    pat2_d = nc.dram_tensor("pat2", [2, 128], F32, kind="ExternalInput").ap()
    pat2T_d = nc.dram_tensor("pat2T", [128, 2], BF16, kind="ExternalInput").ap()
    ones1f_d = nc.dram_tensor("ones1f", [1, 128], F32, kind="ExternalInput").ap()
    epst_d = nc.dram_tensor("epst", [128, 1], F32, kind="ExternalInput").ap()
    zcol_d = nc.dram_tensor("zcol", [128, 1], F32, kind="ExternalInput").ap()
    zrow_d = nc.dram_tensor("zrow", [1, 512], BF16, kind="ExternalInput").ap()
    cosrep_d = nc.dram_tensor("cosrep", [128, S], BF16, kind="ExternalInput").ap()
    sinrep_d = nc.dram_tensor("sinrep", [128, S], BF16, kind="ExternalInput").ap()
    permQ_d = nc.dram_tensor("permQ", [128, 128], BF16, kind="ExternalInput").ap()
    xnT_h = nc.dram_tensor("xnT_h", [C, S], BF16).ap()
    attn_h = nc.dram_tensor("attn_h", [C, S], BF16).ap()
    den_h = nc.dram_tensor("den_h", [128, 512], F32).ap()
    rden_h = nc.dram_tensor("rden_h", [128, 512], F32).ap()

    base_ar = nc.auto_arena()
    s_ident = base_ar.sbuf_tensor("s_ident", [128, 128], BF16)
    s_tri = base_ar.sbuf_tensor("s_tri", [128, 128], BF16)
    s_ones = base_ar.sbuf_tensor("s_ones", [128, 128], BF16)
    s_pat2 = base_ar.sbuf_tensor("s_pat2", [2, 128], F32)
    s_pat2T = base_ar.sbuf_tensor("s_pat2T", [128, 2], BF16)
    s_ones1f = base_ar.sbuf_tensor("s_ones1f", [1, 128], F32)
    s_eps = base_ar.sbuf_tensor("s_eps", [128, 1], F32)
    s_zc = base_ar.sbuf_tensor("s_zc", [128, 1], F32)
    s_zrow = base_ar.sbuf_tensor("s_zrow", [1, 512], BF16)
    s_bqT = base_ar.sbuf_tensor("s_bqT", [128, 8], F32)
    s_bkT = base_ar.sbuf_tensor("s_bkT", [128, 2], F32)
    s_bv16 = base_ar.sbuf_tensor("s_bv16", [1, 256], BF16)
    s_bo = base_ar.sbuf_tensor("s_bo", [1, C], F32)
    s_muR = base_ar.sbuf_tensor("s_muR", [128, 8], F32)
    s_muK = base_ar.sbuf_tensor("s_muK", [128, 8], F32)

    qkv_ar = nc.auto_arena()
    Qbuf = [qkv_ar.sbuf_tensor(f"Qbuf{g}", [128, S], BF16) for g in range(8)]
    Kpk = [qkv_ar.sbuf_tensor(f"Kpk{h}", [128, S], BF16) for h in range(2)]
    Vbuf = qkv_ar.sbuf_tensor("Vbuf", [128, NKV * NT * 65], BF16)
    s_kabs = qkv_ar.sbuf_tensor("s_kabs", [128, 2], F32)
    s_perm = qkv_ar.sbuf_tensor("s_perm", [128, 128], BF16)

    A_ar = nc.auto_arena()
    s_xt = A_ar.sbuf_tensor("s_xt", [128, 4 * C], F32)
    s_stats = A_ar.sbuf_tensor("s_stats", [128, 12], F32)
    s_mv = A_ar.sbuf_tensor("s_mv", [128, 8], F32)
    s_lnt = A_ar.sbuf_tensor("s_lnt", [128, 4], F32)
    s_rstd = A_ar.sbuf_tensor("s_rstd", [128, 4], F32)
    s_negm = A_ar.sbuf_tensor("s_negm", [128, 4], F32)
    s_xn = A_ar.sbuf_tensor("s_xn", [128, 2 * C], BF16)
    s_xnT = A_ar.sbuf_tensor("s_xnT", [128, 2 * 8 * 512], BF16)
    s_ktmp = A_ar.sbuf_tensor("s_ktmp", [128, 1024], BF16)
    s_wq16 = A_ar.sbuf_tensor("s_wq16", [128, 8 * C], BF16)
    s_wk16 = A_ar.sbuf_tensor("s_wk16", [128, 8 * 256], BF16)
    s_wv16 = A_ar.sbuf_tensor("s_wv16", [128, 8 * 256], BF16)

    A2_ar = nc.auto_arena()
    s_cos = A2_ar.sbuf_tensor("s_cos", [128, S], BF16)
    s_sin = A2_ar.sbuf_tensor("s_sin", [128, S], BF16)
    s_rtmp = A2_ar.sbuf_tensor("s_rtmp", [128, 512], BF16)
    s_kacc = A2_ar.sbuf_tensor("s_kacc", [128, 1024], BF16)

    cw_ar = nc.auto_arena()
    s_wo16 = cw_ar.sbuf_tensor("s_wo16", [128, 8 * C], BF16)
    s_wrT = cw_ar.sbuf_tensor("s_wrT", [128, 8 * C], BF16)
    s_wkT = cw_ar.sbuf_tensor("s_wkT", [128, 8 * C], BF16)
    s_wvT = cw_ar.sbuf_tensor("s_wvT", [128, 8 * C], BF16)

    R_ar = nc.auto_arena()
    s_dsb = R_ar.sbuf_tensor("s_dsb", [128, 512], F32)
    s_rsb = R_ar.sbuf_tensor("s_rsb", [128, 512], F32)

    B_ar = nc.auto_arena()
    s_P = B_ar.sbuf_tensor("s_P", [128, 2 * 1536], BF16)
    s_qmx = B_ar.sbuf_tensor("s_qmx", [128, 8], BF16)
    s_prod = B_ar.sbuf_tensor("s_prod", [128, 8], BF16)
    s_sc = B_ar.sbuf_tensor("s_sc", [1, 64], BF16)
    s_btA = B_ar.sbuf_tensor("s_btA", [128, 64], F32)
    s_num = B_ar.sbuf_tensor("s_num", [128, 1024], BF16)
    s_den = B_ar.sbuf_tensor("s_den", [128, 1024], F32)

    C_ar = nc.auto_arena()
    s_xnTc = C_ar.sbuf_tensor("s_xnTc", [128, 2 * 8 * 513], BF16)
    s_attc = C_ar.sbuf_tensor("s_attc", [128, 2 * 8 * 512], BF16)
    s_xr = C_ar.sbuf_tensor("s_xr", [128, 2 * 8 * 512], BF16)
    s_xk = C_ar.sbuf_tensor("s_xk", [128, 2 * 8 * 512], BF16)
    s_dt = C_ar.sbuf_tensor("s_dt", [128, 512], BF16)
    s_rnat = C_ar.sbuf_tensor("s_rnat", [128, 4 * C], BF16)
    s_krelu = C_ar.sbuf_tensor("s_krelu", [128, 512], BF16)
    s_kact = C_ar.sbuf_tensor("s_kact", [128, 2 * 8 * 512], BF16)
    s_xc = C_ar.sbuf_tensor("s_xc", [128, 2 * C], F32)
    s_bob = C_ar.sbuf_tensor("s_bob", [128, C], F32)
    s_rd = C_ar.sbuf_tensor("s_rd", [2, 8 * 512], F32)
    s_f1 = C_ar.sbuf_tensor("s_f1", [128, 2 * 512], F32)
    s_out = C_ar.sbuf_tensor("s_out", [128, 2 * C], F32)

    from contextlib import ExitStack
    ctx = ExitStack()

    def sem(name):
        return ctx.enter_context(nc.semaphore(name))

    LDS = sem("LDS")
    WL = sem("WL")
    LDx = [sem(f"LDx{i}") for i in range(4)]
    LDd = sem("LDd")
    SX = [sem(f"SX{i}") for i in range(2)]
    SB = sem("SB")
    SR = sem("SR")
    LCxn = [sem(f"LCxn{i}") for i in range(2)]
    LCat = [sem(f"LCat{i}") for i in range(2)]
    LCx = [sem(f"LCx{i}") for i in range(2)]
    LCrd = [sem(f"LCrd{i}") for i in range(2)]
    SO = [sem(f"SO{i}") for i in range(2)]
    PEs = sem("PEs")
    ACs = sem("ACs")
    DVs = sem("DVs")
    n = dict(PE=0, AC=0, DV=0)

    block = ctx.enter_context(nc.Block())

    base_ar.allocate(side="left")
    qkv_ar.allocate(side="left")
    A_ar.allocate(side="left")
    A2_ar.allocate(side="right")

    NSETUP = 17

    @block.sync
    def _(e):
        for dst, src in [(s_ident, ident_d), (s_tri, tri_d), (s_ones, onescol_d),
                         (s_pat2, pat2_d), (s_pat2T, pat2T_d),
                         (s_ones1f, ones1f_d), (s_eps, epst_d),
                         (s_zc, zcol_d), (s_zrow, zrow_d), (s_bqT, bqT_d), (s_bkT, bkT_d),
                         (s_bo, bo_d), (s_muR, muR_d), (s_muK, muK_d),
                         (s_cos, cosrep_d), (s_sin, sinrep_d), (s_perm, permQ_d)]:
            e.dma_start(out=dst[:, :], in_=src).then_inc(LDS, 16)

    @block.gpsimd
    def _(e):
        e.dma_start(out=s_wq16[:, :].rearrange("p (cc d) -> p cc d", d=C),
                    in_=wq_d.rearrange("(cc p) d -> p cc d", p=128)).then_inc(WL, 16)
        e.dma_start(out=s_wk16[:, :].rearrange("p (cc d) -> p cc d", d=256),
                    in_=wk_d.rearrange("(cc p) d -> p cc d", p=128)).then_inc(WL, 16)
        e.dma_start(out=s_wv16[:, :].rearrange("p (cc d) -> p cc d", d=256),
                    in_=wv_d.rearrange("(cc p) d -> p cc d", p=128)).then_inc(WL, 16)
        e.dma_start(out=s_bv16[:, :], in_=bv_d).then_inc(WL, 16)
    WL_A = 64

    @block.vector
    def _(e):
        e.wait_ge(LDS, 16 * NSETUP)
        ap = bass.AP(Vbuf, 64, [[Vbuf.shape[1], 128], [65, NKV * NT]])
        e.tensor_copy(ap, s_ones[:, 0:NKV * NT]).then_inc(DVs, 1)
        n['DV'] += 1

    psA = ExitStack()
    ptrA = psA.enter_context(nc.psum_tensor("ptrA", [128, 512], F32))
    ptrB = psA.enter_context(nc.psum_tensor("ptrB", [128, 512], F32))
    pq = psA.enter_context(nc.psum_tensor("pq", [128, 1024], F32))
    pk = psA.enter_context(nc.psum_tensor("pk", [128, 512], F32))
    pv = psA.enter_context(nc.psum_tensor("pv", [128, 512], F32))
    prp = [psA.enter_context(nc.psum_tensor(f"prp{i}", [128, 512], F32))
           for i in range(2)]

    ACH = {}
    nA = min(NCH, _NCHA)
    for ch in range(nA):
        b = ch % 2
        pe0, ac0, dv0 = n['PE'], n['AC'], n['DV']
        stats = [dv0 + 2 * i + 1 for i in range(4)]
        negm = [dv0 + 2 * i + 2 for i in range(4)]
        vcopy = [dv0 + 8 + t + 1 for t in range(4)]
        rope_dv = [dv0 + 12 + u + 1 for u in range(10)]
        rstd_c = [ac0 + 4 * i + 1 for i in range(4)]
        xn_c = [ac0 + 4 * i + 2 for i in range(4)]
        evacA = [ac0 + 4 * i + 3 for i in range(4)]
        evacB = [ac0 + 4 * i + 4 for i in range(4)]
        qevac = [ac0 + 16 + g + 1 for g in range(8)]
        kevac = [ac0 + 24 + g2 + 1 for g2 in range(2)]
        trA = [pe0 + 2 * i + 1 for i in range(4)]
        trB = [pe0 + 2 * i + 2 for i in range(4)]
        qmm = [pe0 + 8 + g + 1 for g in range(8)]
        kmm = [pe0 + 16 + g2 + 1 for g2 in range(2)]
        vmm = [pe0 + 18 + t + 1 for t in range(4)]
        rope_pe = [pe0 + 22 + u + 1 for u in range(10)]
        ACH[ch] = dict(xn=xn_c, vmm=vmm, evacA=evacA, evacB=evacB,
                       vcopy=vcopy, qevac=qevac, kevac=kevac, trB=trB,
                       rope=rope_dv, rope_pe=rope_pe)

        @block.sync
        def _(e, ch=ch):
            for i in range(4):
                st = 4 * ch + i
                if ch >= 1:
                    e.wait_ge(ACs, ACH[ch - 1]['xn'][i])
                e.dma_start(out=s_xt[:, i * C:(i + 1) * C],
                            in_=x_d[st * 128:(st + 1) * 128, :]
                            ).then_inc(LDx[i], 16)

        @block.vector
        def _(e, ch=ch, b=b, vmm=vmm, rstd_c=rstd_c):
            for i in range(4):
                e.wait_ge(LDx[i], 16 * (ch + 1))
                xt = s_xt[:, i * C:(i + 1) * C]
                e.bn_stats(s_stats[:, 0:6], xt[:, 0:512])
                e.bn_stats(s_stats[:, 6:12], xt[:, 512:1024])
                e.drain()
                e.bn_aggr(s_mv[:, i * 2:i * 2 + 2], s_stats[:, :])
                e.drain().then_inc(DVs, 1)
                e.wait_ge(ACs, rstd_c[i])
                e.tensor_scalar(s_negm[:, i:i + 1], s_mv[:, i * 2:i * 2 + 1],
                                s_rstd[:, i:i + 1], -1.0,
                                op0=ALU.mult, op1=ALU.mult).then_inc(DVs, 1)
            for t in range(4):
                e.wait_ge(PEs, vmm[t])
                st = 4 * ch + t
                for kv in range(4):
                    ins = e.tensor_copy(
                        Vbuf[:, (kv * NT + st) * 65:(kv * NT + st) * 65 + 64],
                        pv[:, kv * 64:(kv + 1) * 64])
                ins.then_inc(DVs, 1)
            cosl = s_cos[:, ch * 512:(ch + 1) * 512]
            sinl = s_sin[:, ch * 512:(ch + 1) * 512]
            for u in range(10):
                bank = prp[u % 2]
                e.wait_ge(PEs, rope_pe[u])
                if u < 8:
                    tgt = Qbuf[u][:, ch * 512:(ch + 1) * 512]
                    e.tensor_tensor(s_rtmp[:, :], bank[:, :], sinl, op=ALU.mult)
                    e.tensor_tensor(tgt, tgt, cosl, op=ALU.mult)
                    e.tensor_tensor(tgt, tgt, s_rtmp[:, :],
                                    op=ALU.add).then_inc(DVs, 1)
                else:
                    g2 = u - 8
                    tgt = Kpk[g2][:, ch * 512:(ch + 1) * 512]
                    acc = s_kacc[:, g2 * 512:(g2 + 1) * 512]
                    e.tensor_tensor(s_rtmp[:, :], bank[:, :], sinl, op=ALU.mult)
                    e.tensor_tensor(tgt, s_ktmp[:, g2 * 512:(g2 + 1) * 512],
                                    cosl, op=ALU.mult)
                    e.tensor_tensor(tgt, tgt, s_rtmp[:, :], op=ALU.add)
                    e.tensor_scalar(s_rtmp[:, :], tgt, -1.0, None, op0=ALU.mult)
                    if ch == 0:
                        e.tensor_tensor(acc, tgt, s_rtmp[:, :],
                                        op=ALU.max).then_inc(DVs, 1)
                    else:
                        e.tensor_tensor(acc, acc, tgt, op=ALU.max)
                        e.tensor_tensor(acc, acc, s_rtmp[:, :],
                                        op=ALU.max).then_inc(DVs, 1)
        n['DV'] += 22

        @block.scalar
        def _(e, ch=ch, b=b, stats=stats, negm=negm, trA=trA, trB=trB,
              qmm=qmm, kmm=kmm):
            for i in range(4):
                st = 4 * ch + i
                sl = st % 2
                e.wait_ge(DVs, stats[i])
                e.activation(s_lnt[:, i:i + 1], s_mv[:, i * 2 + 1:i * 2 + 2],
                             AF.Ln, bias=s_eps[:, :])
                e.activation(s_rstd[:, i:i + 1], s_lnt[:, i:i + 1],
                             AF.Exp, scale=-0.5).then_inc(ACs, 1)
                e.wait_ge(DVs, negm[i])
                if st >= 2:
                    e.wait_ge(PEs, ACH[(st - 2) // 4]['trB'][(st - 2) % 4])
                e.activation(s_xn[:, sl * C:(sl + 1) * C], s_xt[:, i * C:(i + 1) * C],
                             AF.Identity, bias=s_negm[:, i:i + 1],
                             scale=s_rstd[:, i:i + 1]).then_inc(ACs, 1)
                e.wait_ge(PEs, trA[i])
                if ch >= 2 and i == 0:
                    e.wait_ge(PEs, ACH[ch - 2]['vmm'][3])
                    e.wait_ge(SX[b], 16 * 8 * (ch // 2))
                for g in range(4):
                    ins = e.activation(
                        s_xnT[:, (b * 8 + g) * 512 + i * 128:(b * 8 + g) * 512 + (i + 1) * 128],
                        ptrA[:, g * 128:(g + 1) * 128], AF.Identity)
                ins.then_inc(ACs, 1)
                e.wait_ge(PEs, trB[i])
                for g in range(4):
                    ins = e.activation(
                        s_xnT[:, (b * 8 + 4 + g) * 512 + i * 128:(b * 8 + 4 + g) * 512 + (i + 1) * 128],
                        ptrB[:, g * 128:(g + 1) * 128], AF.Identity)
                ins.then_inc(ACs, 1)
            for g in range(8):
                e.wait_ge(PEs, qmm[g])
                e.activation(Qbuf[g][:, ch * 512:(ch + 1) * 512],
                             pq[:, (g % 2) * 512:(g % 2 + 1) * 512],
                             AF.Identity, bias=s_bqT[:, g:g + 1]).then_inc(ACs, 1)
            for g2 in range(2):
                e.wait_ge(PEs, kmm[g2])
                if ch >= 1:
                    e.wait_ge(DVs, ACH[ch - 1]['rope'][8 + g2])
                e.activation(s_ktmp[:, g2 * 512:(g2 + 1) * 512], pk[:, :],
                             AF.Identity, bias=s_bkT[:, g2:g2 + 1]).then_inc(ACs, 1)
        n['AC'] += 26

        @block.gpsimd
        def _(e, ch=ch, b=b, evacB=evacB):
            e.wait_ge(ACs, evacB[3])
            for g in range(8):
                e.dma_start(out=xnT_h[g * 128:(g + 1) * 128, ch * 512:(ch + 1) * 512],
                            in_=s_xnT[:, (b * 8 + g) * 512:(b * 8 + g + 1) * 512]
                            ).then_inc(SX[b], 16)

        @block.tensor
        def _(e, ch=ch, b=b, xn_c=xn_c, evacA=evacA, evacB=evacB,
              qevac=qevac, kevac=kevac, vcopy=vcopy):
            for i in range(4):
                sl = (4 * ch + i) % 2
                e.wait_ge(ACs, xn_c[i])
                if i > 0:
                    e.wait_ge(ACs, evacA[i - 1])
                elif ch > 0:
                    e.wait_ge(ACs, ACH[ch - 1]['evacA'][3])
                for g in range(4):
                    ins = e.matmul(ptrA[:, g * 128:(g + 1) * 128],
                                   s_xn[:, sl * C + g * 128:sl * C + (g + 1) * 128],
                                   s_ident[:, :], start=(g == 0), stop=(g == 3))
                ins.then_inc(PEs, 1)
                if i > 0:
                    e.wait_ge(ACs, evacB[i - 1])
                elif ch > 0:
                    e.wait_ge(ACs, ACH[ch - 1]['evacB'][3])
                for g in range(4):
                    ins = e.matmul(ptrB[:, g * 128:(g + 1) * 128],
                                   s_xn[:, sl * C + (4 + g) * 128:sl * C + (5 + g) * 128],
                                   s_ident[:, :], start=(g == 0), stop=(g == 3))
                ins.then_inc(PEs, 1)
            e.wait_ge(ACs, evacB[3])
            if ch == 0:
                e.wait_ge(WL, WL_A)
            for g in range(8):
                if g >= 2:
                    e.wait_ge(ACs, qevac[g - 2])
                elif ch > 0:
                    e.wait_ge(ACs, ACH[ch - 1]['qevac'][6 + g])
                for cc in range(8):
                    ins = e.matmul(pq[:, (g % 2) * 512:(g % 2 + 1) * 512],
                                   s_wq16[:, cc * C + g * 128:cc * C + (g + 1) * 128],
                                   s_xnT[:, (b * 8 + cc) * 512:(b * 8 + cc + 1) * 512],
                                   start=(cc == 0), stop=(cc == 7))
                ins.then_inc(PEs, 1)
            for g2 in range(2):
                if g2 >= 1:
                    e.wait_ge(ACs, kevac[0])
                elif ch > 0:
                    e.wait_ge(ACs, ACH[ch - 1]['kevac'][1])
                for cc in range(8):
                    ins = e.matmul(pk[:, :],
                                   s_wk16[:, cc * 256 + g2 * 128:cc * 256 + (g2 + 1) * 128],
                                   s_xnT[:, (b * 8 + cc) * 512:(b * 8 + cc + 1) * 512],
                                   start=(cc == 0), stop=(cc == 7))
                ins.then_inc(PEs, 1)
            for t in range(4):
                if t >= 1:
                    e.wait_ge(DVs, vcopy[t - 1])
                elif ch >= 1:
                    e.wait_ge(DVs, ACH[ch - 1]['vcopy'][3])
                for cc in range(8):
                    e.matmul(pv[:, 0:256],
                             s_xnT[:, (b * 8 + cc) * 512 + t * 128:(b * 8 + cc) * 512 + (t + 1) * 128],
                             s_wv16[:, cc * 256:(cc + 1) * 256],
                             start=(cc == 0), stop=False)
                e.matmul(pv[:, 0:256], s_ones[0:1, 0:128], s_bv16[:, :],
                         start=False, stop=True).then_inc(PEs, 1)
            for u in range(10):
                if u >= 2:
                    e.wait_ge(DVs, rope_dv[u - 2])
                elif ch > 0:
                    e.wait_ge(DVs, ACH[ch - 1]['rope'][8 + u])
                if u < 8:
                    e.wait_ge(ACs, qevac[u])
                    src = Qbuf[u][:, ch * 512:(ch + 1) * 512]
                else:
                    e.wait_ge(ACs, kevac[u - 8])
                    src = s_ktmp[:, (u - 8) * 512:(u - 7) * 512]
                e.matmul(prp[u % 2][:, :], s_perm[:, :], src,
                         start=True, stop=True).then_inc(PEs, 1)
        n['PE'] += 32

    @block.vector
    def _(e):
        for g2 in range(2):
            e.reduce_max(s_kabs[:, g2:g2 + 1],
                         s_kacc[:, g2 * 512:(g2 + 1) * 512], axis=AX.X)
        e.drain()
        e.tensor_scalar(s_kabs[:, 0:2], s_kabs[:, 0:2], 0.125, None,
                        op0=ALU.mult).then_inc(DVs, 1)
    n['DV'] += 1
    A_done = dict(n)

    store_state = dict(SXa=16 * 8 * ((nA + 1) // 2), SXb=16 * 8 * (nA // 2),
                       SB=0, SR=0, SOa=0, SOb=0)

    def store_totals():
        return [(SX[0], store_state['SXa']), (SX[1], store_state['SXb']),
                (SB, store_state['SB']), (SR, store_state['SR']),
                (SO[0], store_state['SOa']), (SO[1], store_state['SOb'])]

    def _finish(open_psums, open_left, open_right):
        @block.scalar
        def _(e):
            for s, tot in store_totals():
                if tot:
                    e.wait_ge(s, tot)
        for ps in open_psums:
            ps.close()
        for ar in open_left:
            ar.free()
        for ar in open_right:
            ar.free()
        base_ar.free()
        ctx.close()
        return nc

    if lvl == 0:
        return _finish([psA], [A_ar, qkv_ar], [A2_ar])

    A2_dv = n['DV']

    psA.close()
    A_ar.free()
    A2_ar.free()
    cw_ar.allocate(side="right")
    B_ar.allocate(side="left")

    if lvl == 1:
        return _finish([], [B_ar, qkv_ar], [cw_ar])

    @block.gpsimd
    def _(e):
        e.wait_ge(DVs, A2_dv)
        e.dma_start(out=s_wo16[:, :].rearrange("p (g d) -> p g d", d=C),
                    in_=wo_d.rearrange("(g p) d -> p g d", p=128)).then_inc(WL, 16)
        for dst, src in [(s_wrT, wrT_d), (s_wkT, wkT_d), (s_wvT, wvT_d)]:
            e.dma_start(out=dst[:, :].rearrange("p (cc d) -> p cc d", d=C),
                        in_=src.rearrange("(cc p) d -> p cc d", p=128)).then_inc(WL, 16)
    WL_ALL = WL_A + 64

    psB = ExitStack()
    pS = psB.enter_context(nc.psum_tensor("pS", [128, 3072], F32))
    pAV = psB.enter_context(nc.psum_tensor("pAV", [128, 1024], F32))

    @block.tensor
    def _(e):
        e.wait_ge(LDS, 16 * NSETUP)
        e.wait_ge(PEs, A_done['PE'])
        for k2 in range(6):
            ins = e.matmul(pS[:, k2 * 512:(k2 + 1) * 512], s_ones[0:1, :],
                           s_zrow[:, :], start=True, stop=True)
        ins.then_inc(PEs, 1)
    n['PE'] += 1

    bq_dv0 = n['DV']
    bq_pe0 = n['PE']

    @block.vector
    def _(e):
        e.wait_ge(PEs, A_done['PE'])
        e.wait_ge(ACs, A_done['AC'])
        for pr in range(8):
            cpx = pr // 4
            for qt in range(NCH):
                e.reduce_max(s_qmx[:, qt:qt + 1],
                             Qbuf[pr][:, qt * 512:(qt + 1) * 512], axis=AX.X,
                             apply_absolute_value=True)
            e.drain()
            ins = None
            for qt in range(NCH):
                ins = e.tensor_scalar(s_prod[:, qt:qt + 1], s_qmx[:, qt:qt + 1],
                                      s_kabs[:, cpx:cpx + 1], -1.0,
                                      op0=ALU.mult, op1=ALU.mult)
            ins.then_inc(DVs, 1)
            e.wait_ge(PEs, bq_pe0 + pr + 1)
            e.tensor_copy(s_sc[0:1, pr * 8:(pr + 1) * 8],
                          pAV[0:1, 0:8]).then_inc(DVs, 1)
        e.wait_ge(PEs, bq_pe0 + 9)
        e.tensor_copy(s_btA[:, 0:64], pAV[:, 512:576]).then_inc(DVs, 1)
    n['DV'] += 17
    dv_btA = n['DV']

    @block.tensor
    def _(e):
        for pr in range(8):
            e.wait_ge(DVs, bq_dv0 + 2 * pr + 1)
            if pr > 0:
                e.wait_ge(DVs, bq_dv0 + 2 * pr)
            e.matmul(pAV[0:1, 0:8], s_pat2T[:, 0:1], s_prod[:, 0:8],
                     start=True, stop=True).then_inc(PEs, 1)
        e.wait_ge(DVs, bq_dv0 + 16)
        e.matmul(pAV[:, 512:576], s_ones[0:1, :], s_sc[0:1, 0:64],
                 start=True, stop=True).then_inc(PEs, 1)
    n['PE'] += 9

    last_exp = 0
    avevac_prev = 0
    pair_serial = 0

    for qt in range(NCH):
        for pr in range(8):
            cpr = pr // 4
            nfull = 4 * qt
            units = [('f', t, hh) for t in range(nfull) for hh in range(2)] \
                  + [('d', kk, hh) for kk in range(4) for hh in range(2)]
            nu = len(units)
            batches = [units[i:i + 3] for i in range(0, nu, 3)]
            nb = len(batches)
            pe0, ac0, dv0 = n['PE'], n['AC'], n['DV']

            pe_scores = [pe0 + 1]
            for i in range(1, nb):
                pe_scores.append(pe0 + 2 * i)
            pe_avs = [pe0 + 2 * i + 3 for i in range(nb - 1)]
            pe_avs.append(pe0 + 2 * nb)
            ac_exp = [ac0 + 1 + i for i in range(nb)]
            dv_masks = {}
            dvc = dv0
            for i, bt in enumerate(batches):
                if any(u[0] == 'd' for u in bt):
                    dvc += 1
                    dv_masks[i] = dvc
            dv_avevac = dvc + 1
            par = pair_serial % 2

            @block.vector
            def _(e, qt=qt, pr=pr, batches=batches, pe_avs=pe_avs,
                  ac_exp=ac_exp, dv_masks=dv_masks, pair_serial=pair_serial):
                for i, bt in enumerate(batches):
                    if not any(u[0] == 'd' for u in bt):
                        continue
                    e.wait_ge(ACs, ac_exp[i])
                    ins = None
                    for pos, u in enumerate(bt):
                        if u[0] != 'd':
                            continue
                        kk = u[1]
                        base = (i % 2) * 1536 + pos * 512 + kk * 128
                        ins = e.tensor_tensor(s_P[:, base:base + 128],
                                              s_P[:, base:base + 128],
                                              s_tri[:, :], op=ALU.mult)
                    ins.then_inc(DVs, 1)
                e.wait_ge(PEs, pe_avs[-1])
                if pair_serial >= 1:
                    e.wait_ge(SB, 64 * pair_serial)
                e.tensor_copy(s_num[0:64, 0:512], pAV[0:64, 0:512])
                e.tensor_copy(s_num[0:64, 512:1024], pAV[0:64, 512:1024])
                e.tensor_copy(s_den[64:65, 0:512], pAV[64:65, 0:512])
                ins = e.tensor_copy(s_den[64:65, 512:1024], pAV[64:65, 512:1024])
                ins.then_inc(DVs, 1)
            n['DV'] = dv_avevac

            @block.scalar
            def _(e, qt=qt, pr=pr, batches=batches,
                  pe_scores=pe_scores):
                for i, bt in enumerate(batches):
                    e.wait_ge(PEs, pe_scores[i])
                    if i == 0 and qt == 0:
                        e.wait_ge(DVs, dv_btA)
                    segs = []
                    run0 = None
                    for pos, u in enumerate(bt):
                        if u[0] == 'f':
                            if run0 is None:
                                run0 = pos * 512
                        else:
                            if run0 is not None:
                                segs.append((run0, pos * 512 - run0))
                                run0 = None
                            kk = u[1]
                            segs.append((pos * 512 + kk * 128, 512 - kk * 128))
                    if run0 is not None:
                        segs.append((run0, len(bt) * 512 - run0))
                    base = (i % 2) * 1536
                    ins = None
                    for s0, wd in segs:
                        ins = e.activation(s_P[:, base + s0:base + s0 + wd],
                                           pS[:, base + s0:base + s0 + wd],
                                           AF.Exp, scale=0.125,
                                           bias=s_btA[:, pr * 8 + qt:pr * 8 + qt + 1])
                    ins.then_inc(ACs, 1)
            n['AC'] = ac_exp[-1]

            @block.gpsimd
            def _(e, qt=qt, pr=pr, dv_avevac=dv_avevac):
                e.wait_ge(DVs, dv_avevac)
                he, ho = 2 * pr, 2 * pr + 1
                e.dma_start(out=attn_h[he * 64:(he + 1) * 64, qt * 512:(qt + 1) * 512],
                            in_=s_num[0:64, 0:512]).then_inc(SB, 16)
                e.dma_start(out=attn_h[ho * 64:(ho + 1) * 64, qt * 512:(qt + 1) * 512],
                            in_=s_num[0:64, 512:1024]).then_inc(SB, 16)
                e.dma_start(out=den_h[qt * 16 + he:qt * 16 + he + 1, :],
                            in_=s_den[64:65, 0:512]).then_inc(SB, 16)
                e.dma_start(out=den_h[qt * 16 + ho:qt * 16 + ho + 1, :],
                            in_=s_den[64:65, 512:1024]).then_inc(SB, 16)

            @block.tensor
            def _(e, qt=qt, pr=pr, cpr=cpr, batches=batches,
                  last_exp=last_exp, avevac_prev=avevac_prev,
                  dv_masks=dv_masks, ac_exp=ac_exp, nb=nb,
                  pair_serial=pair_serial):
                e.wait_ge(ACs, last_exp)
                started = {0: False, 1: False}

                def emit_scores(i, bt):
                    if i >= 2:
                        e.wait_ge(ACs, ac_exp[i - 2])
                    ins = None
                    for pos, u in enumerate(bt):
                        hh = u[2]
                        base = 64 * hh
                        slot = (i % 2) * 1536 + pos * 512
                        if u[0] == 'f':
                            t = u[1]
                            ins = e.matmul(pS[:, slot:slot + 512],
                                           Kpk[cpr][base:base + 64, t * 128:(t + 1) * 128],
                                           Qbuf[pr][base:base + 64, qt * 512:(qt + 1) * 512],
                                           start=True, stop=True, tile_position=(base, 0))
                        else:
                            kk = u[1]
                            c0 = qt * 512 + kk * 128
                            ins = e.matmul(pS[:, slot + kk * 128:slot + 512],
                                           Kpk[cpr][base:base + 64, c0:c0 + 128],
                                           Qbuf[pr][base:base + 64, c0:(qt + 1) * 512],
                                           start=True, stop=True, tile_position=(base, 0))
                    ins.then_inc(PEs, 1)

                def emit_avs(i, bt):
                    if i == 0:
                        e.wait_ge(DVs, dv_btA if pair_serial == 0 else avevac_prev)
                    e.wait_ge(ACs, ac_exp[i])
                    if i in dv_masks:
                        e.wait_ge(DVs, dv_masks[i])
                    ins = None
                    for pos, u in enumerate(bt):
                        hh = u[2]
                        t = u[1] if u[0] == 'f' else qt * 4 + u[1]
                        slot = (i % 2) * 1536 + pos * 512
                        off = 0 if u[0] == 'f' else u[1] * 128
                        is_last = (u[0] == 'd' and u[1] == 3)
                        ins = e.matmul(pAV[0:65, hh * 512 + off:hh * 512 + 512],
                                       Vbuf[:, ((2 * cpr + hh) * NT + t) * 65:((2 * cpr + hh) * NT + t) * 65 + 65],
                                       s_P[:, slot + off:slot + 512],
                                       start=(not started[hh]), stop=is_last)
                        started[hh] = True
                    ins.then_inc(PEs, 1)

                emit_scores(0, batches[0])
                for i in range(1, nb):
                    emit_scores(i, batches[i])
                    emit_avs(i - 1, batches[i - 1])
                emit_avs(nb - 1, batches[nb - 1])
            n['PE'] = pe_avs[-1]
            last_exp = ac_exp[-1]
            avevac_prev = dv_avevac
            pair_serial += 1

    B_done = dict(n)
    store_state['SB'] = 64 * 64

    if lvl == 2:
        return _finish([psB], [B_ar, qkv_ar], [cw_ar])

    R_ar.allocate(side="right")

    @block.sync
    def _(e):
        e.wait_ge(SB, store_state['SB'])
        e.dma_start(out=s_dsb[:, :], in_=den_h).then_inc(LDd, 16)

    @block.vector
    def _(e):
        e.wait_ge(LDd, 16)
        e.reciprocal(s_rsb[:, :], s_dsb[:, :]).then_inc(DVs, 1)
    n['DV'] += 1
    dv_recip = n['DV']

    @block.gpsimd
    def _(e):
        e.wait_ge(DVs, dv_recip)
        e.dma_start(out=rden_h, in_=s_rsb[:, :]).then_inc(SR, 16)
    store_state['SR'] = 16

    psB.close()
    W_done = dict(n)

    if lvl == 3:
        return _finish([], [B_ar, qkv_ar], [R_ar, cw_ar])

    B_ar.free()
    qkv_ar.free()
    R_ar.free()
    C_ar.allocate(side="left")

    psC = ExitStack()
    pbc = psC.enter_context(nc.psum_tensor("pbc", [128, 512], F32))
    pkC = psC.enter_context(nc.psum_tensor("pkC", [128, 512], F32))
    prC = psC.enter_context(nc.psum_tensor("prC", [128, 1024], F32))
    pvC = psC.enter_context(nc.psum_tensor("pvC", [128, 1024], F32))
    pwC = psC.enter_context(nc.psum_tensor("pwC", [128, 1024], F32))

    @block.tensor
    def _(e):
        e.wait_ge(LDS, 16 * NSETUP)
        e.wait_ge(DVs, W_done['DV'])
        e.wait_ge(ACs, B_done['AC'])
        e.matmul(pwC[:, 0:512], s_ones1f[:, :], s_bo[:, 0:512], start=True, stop=True)
        e.matmul(pwC[:, 512:1024], s_ones1f[:, :], s_bo[:, 512:1024],
                 start=True, stop=True).then_inc(PEs, 1)
    n['PE'] += 1
    pe_bob = n['PE']

    @block.vector
    def _(e):
        e.wait_ge(PEs, pe_bob)
        e.tensor_copy(s_bob[:, :], pwC[:, :]).then_inc(DVs, 1)
    n['DV'] += 1
    dv_bob = n['DV']

    CCH = {}
    nC = min(NCH, _NCHC)
    for ch in range(nC):
        b = ch % 2
        pe0, ac0, dv0 = n['PE'], n['AC'], n['DV']
        dvc = dv0
        if ch == 0:
            dvc += 1
            mix_ct = []
            for g in range(8):
                dvc += 1
                mix_ct.append(dvc)
        else:
            mix_ct = CCH[ch - 1]['mixT']
        attnN_ct = []
        for g in range(8):
            dvc += 1
            attnN_ct.append(dvc)
        kact_ct = []
        for oc in range(8):
            dvc += 1
            kact_ct.append(dvc)
        xplus_ct = []
        final_ct = {}
        for t in range(4):
            dvc += 1
            xplus_ct.append(dvc)
            for dh in range(2):
                final_ct[(t, dh)] = dvc + 3
                dvc += 3
        mixT_ct = []
        if ch < min(NCH, _NCHC) - 1:
            for g in range(8):
                dvc += 1
                mixT_ct.append(dvc)
        pec = pe0
        bc_ct = []
        for g in range(8):
            pec += 1
            bc_ct.append(pec)
        kmm_ct = []
        for oc in range(8):
            pec += 1
            kmm_ct.append(pec)
        rmm_ct = {}
        vmm_ct = {}
        womm_ct = {}
        for t in range(4):
            for dh in range(2):
                pec += 1
                rmm_ct[(t, dh)] = pec
            for dh in range(2):
                pec += 1
                vmm_ct[(t, dh)] = pec
            for dh in range(2):
                pec += 1
                womm_ct[(t, dh)] = pec
        acc = ac0
        krelu_ct = []
        for oc in range(8):
            acc += 1
            krelu_ct.append(acc)
        sig_ct = {}
        for t in range(4):
            for dh in range(2):
                acc += 1
                sig_ct[(t, dh)] = acc
        CCH[ch] = dict(final=final_ct, mix=mix_ct, mixT=mixT_ct, attnN=attnN_ct,
                       kact=kact_ct, xplus=xplus_ct, sig=sig_ct, bc=bc_ct,
                       krelu=krelu_ct, last_pe=pec, last_dv=dvc)

        @block.sync
        def _(e, ch=ch, b=b):
            if ch == 0:
                e.wait_ge(PEs, W_done['PE'])
                e.wait_ge(DVs, W_done['DV'])
                e.wait_ge(ACs, B_done['AC'])
                e.wait_ge(SR, 16)
            for g in range(8):
                if ch >= 1:
                    e.wait_ge(PEs, CCH[ch - 1]['bc'][g])
                e.dma_start(out=s_rd[0:2, g * 512:(g + 1) * 512],
                            in_=rden_h[ch * 16 + 2 * g:ch * 16 + 2 * g + 2, :]
                            ).then_inc(LCrd[g % 2], 16)
            if ch >= 2:
                e.wait_ge(PEs, CCH[ch - 2]['last_pe'])
                e.wait_ge(DVs, CCH[ch - 2]['last_dv'])
            for g in range(8):
                if ch == 0:
                    dst = s_xnTc[:, (b * 8 + g) * 513 + 1:(b * 8 + g) * 513 + 513]
                    src = xnT_h[g * 128:(g + 1) * 128, 0:512]
                else:
                    dst = s_xnTc[:, (b * 8 + g) * 513:(b * 8 + g + 1) * 513]
                    src = xnT_h[g * 128:(g + 1) * 128, ch * 512 - 1:(ch + 1) * 512]
                e.dma_start(out=dst, in_=src).then_inc(LCxn[b], 16)
            for g in range(8):
                e.dma_start(out=s_attc[:, (b * 8 + g) * 512:(b * 8 + g + 1) * 512],
                            in_=attn_h[g * 128:(g + 1) * 128, ch * 512:(ch + 1) * 512]
                            ).then_inc(LCat[b], 16)

        @block.gpsimd
        def _(e, ch=ch, final_ct=final_ct):
            for t in range(4):
                tg = ch * 4 + t
                if tg >= 2:
                    prev_ch, prev_t = (tg - 2) // 4, (tg - 2) % 4
                    e.wait_ge(DVs, CCH[prev_ch]['final'][(prev_t, 1)])
                e.dma_start(out=s_xc[:, (tg % 2) * C:(tg % 2 + 1) * C],
                            in_=x_d[tg * 128:(tg + 1) * 128, :]
                            ).then_inc(LCx[tg % 2], 16)
            for t in range(4):
                tg = ch * 4 + t
                e.wait_ge(DVs, final_ct[(t, 1)])
                e.dma_start(out=out_d[tg * 128:(tg + 1) * 128, :],
                            in_=s_out[:, (tg % 2) * C:(tg % 2 + 1) * C]
                            ).then_inc(SO[tg % 2], 16)

        def emit_mix(e, tch):
            bt = tch % 2
            e.wait_ge(LCxn[bt], 16 * 8 * (tch // 2 + 1))
            if tch == 0:
                ins = None
                for g in range(8):
                    ap = s_xnTc[:, (bt * 8 + g) * 513:(bt * 8 + g) * 513 + 1]
                    ins = e.tensor_copy(ap, s_zc[:, :])
                ins.then_inc(DVs, 1)
            if tch >= 2:
                e.wait_ge(PEs, CCH[tch - 2]['last_pe'])
            for g in range(8):
                cur = s_xnTc[:, (bt * 8 + g) * 513 + 1:(bt * 8 + g) * 513 + 513]
                prev = s_xnTc[:, (bt * 8 + g) * 513:(bt * 8 + g) * 513 + 512]
                e.tensor_tensor(s_dt[:, :], cur, prev, op=ALU.subtract)
                e.scalar_tensor_tensor(s_xr[:, (bt * 8 + g) * 512:(bt * 8 + g + 1) * 512],
                                       s_dt[:, :], s_muR[:, g:g + 1], prev,
                                       op0=ALU.mult, op1=ALU.add)
                e.scalar_tensor_tensor(s_xk[:, (bt * 8 + g) * 512:(bt * 8 + g + 1) * 512],
                                       s_dt[:, :], s_muK[:, g:g + 1], prev,
                                       op0=ALU.mult, op1=ALU.add).then_inc(DVs, 1)

        @block.vector
        def _(e, ch=ch, b=b, bc_ct=bc_ct, vmm_ct=vmm_ct, womm_ct=womm_ct,
              krelu_ct=krelu_ct, sig_ct=sig_ct, emit_mix=emit_mix):
            if ch == 0:
                emit_mix(e, 0)
            e.wait_ge(LCat[b], 16 * 8 * (ch // 2 + 1))
            for g in range(8):
                e.wait_ge(PEs, bc_ct[g])
                e.tensor_tensor(s_attc[:, (b * 8 + g) * 512:(b * 8 + g + 1) * 512],
                                s_attc[:, (b * 8 + g) * 512:(b * 8 + g + 1) * 512],
                                pbc[:, :], op=ALU.mult).then_inc(DVs, 1)
            for oc in range(8):
                e.wait_ge(ACs, krelu_ct[oc])
                e.tensor_tensor(s_kact[:, (b * 8 + oc) * 512:(b * 8 + oc + 1) * 512],
                                s_krelu[:, :], s_krelu[:, :],
                                op=ALU.mult).then_inc(DVs, 1)
            for t in range(4):
                tg = ch * 4 + t
                e.wait_ge(LCx[tg % 2], 16 * (tg // 2 + 1))
                if ch == 0 and t == 0:
                    e.wait_ge(DVs, dv_bob)
                e.tensor_tensor(s_xc[:, (tg % 2) * C:(tg % 2 + 1) * C],
                                s_xc[:, (tg % 2) * C:(tg % 2 + 1) * C],
                                s_bob[:, :], op=ALU.add).then_inc(DVs, 1)
                for dh in range(2):
                    e.wait_ge(PEs, vmm_ct[(t, dh)])
                    e.wait_ge(ACs, sig_ct[(t, dh)])
                    e.tensor_tensor(s_f1[:, dh * 512:(dh + 1) * 512],
                                    s_rnat[:, t * C + dh * 512:t * C + (dh + 1) * 512],
                                    pvC[:, dh * 512:(dh + 1) * 512],
                                    op=ALU.mult).then_inc(DVs, 1)
                    e.wait_ge(PEs, womm_ct[(t, dh)])
                    e.tensor_tensor(s_f1[:, dh * 512:(dh + 1) * 512],
                                    s_f1[:, dh * 512:(dh + 1) * 512],
                                    pwC[:, dh * 512:(dh + 1) * 512],
                                    op=ALU.add).then_inc(DVs, 1)
                    if tg >= 2 and dh == 0:
                        e.wait_ge(SO[tg % 2], 16 * (tg // 2))
                    e.tensor_tensor(s_out[:, (tg % 2) * C + dh * 512:(tg % 2) * C + (dh + 1) * 512],
                                    s_f1[:, dh * 512:(dh + 1) * 512],
                                    s_xc[:, (tg % 2) * C + dh * 512:(tg % 2) * C + (dh + 1) * 512],
                                    op=ALU.add).then_inc(DVs, 1)
            if ch < nC - 1:
                emit_mix(e, ch + 1)
        n['DV'] = dvc

        @block.scalar
        def _(e, ch=ch, b=b, kmm_ct=kmm_ct, rmm_ct=rmm_ct, final_ct=final_ct):
            for oc in range(8):
                e.wait_ge(PEs, kmm_ct[oc])
                if oc >= 1:
                    e.wait_ge(DVs, CCH[ch]['kact'][oc - 1])
                elif ch >= 1:
                    e.wait_ge(DVs, CCH[ch - 1]['kact'][7])
                e.activation(s_krelu[:, :], pkC[:, :], AF.Relu).then_inc(ACs, 1)
            for t in range(4):
                for dh in range(2):
                    e.wait_ge(PEs, rmm_ct[(t, dh)])
                    if ch >= 1:
                        e.wait_ge(DVs, CCH[ch - 1]['final'][(t, dh)] - 2)
                    e.activation(s_rnat[:, t * C + dh * 512:t * C + (dh + 1) * 512],
                                 prC[:, dh * 512:(dh + 1) * 512],
                                 AF.Sigmoid).then_inc(ACs, 1)
        n['AC'] = acc

        @block.tensor
        def _(e, ch=ch, b=b, mix_ct=mix_ct, attnN_ct=attnN_ct,
              kact_ct=kact_ct, krelu_ct=krelu_ct, sig_ct=sig_ct,
              final_ct=final_ct):
            if ch == 0:
                e.wait_ge(WL, WL_ALL)
                e.wait_ge(DVs, W_done['DV'])
            for g in range(8):
                if g == 0:
                    e.wait_ge(LCrd[0], 16 * 4 * (ch + 1))
                    e.wait_ge(LCrd[1], 16 * 4 * (ch + 1))
                if g >= 1:
                    e.wait_ge(DVs, attnN_ct[g - 1])
                elif ch >= 1:
                    e.wait_ge(DVs, CCH[ch - 1]['attnN'][7])
                e.matmul(pbc[:, :], s_pat2[:, :],
                         s_rd[0:2, g * 512:(g + 1) * 512],
                         start=True, stop=True).then_inc(PEs, 1)
            e.wait_ge(DVs, mix_ct[7])
            for oc in range(8):
                if oc >= 1:
                    e.wait_ge(ACs, krelu_ct[oc - 1])
                elif ch >= 1:
                    e.wait_ge(ACs, CCH[ch - 1]['krelu'][7])
                for cc in range(8):
                    ins = e.matmul(pkC[:, :],
                                   s_wkT[:, cc * C + oc * 128:cc * C + (oc + 1) * 128],
                                   s_xk[:, (b * 8 + cc) * 512:(b * 8 + cc + 1) * 512],
                                   start=(cc == 0), stop=(cc == 7))
                ins.then_inc(PEs, 1)
            for t in range(4):
                for dh in range(2):
                    if t > 0:
                        e.wait_ge(ACs, sig_ct[(t - 1, dh)])
                    elif ch > 0:
                        e.wait_ge(ACs, CCH[ch - 1]['sig'][(3, dh)])
                    for cc in range(8):
                        ins = e.matmul(prC[:, dh * 512:(dh + 1) * 512],
                                       s_xr[:, (b * 8 + cc) * 512 + t * 128:(b * 8 + cc) * 512 + (t + 1) * 128],
                                       s_wrT[:, cc * C + dh * 512:cc * C + (dh + 1) * 512],
                                       start=(cc == 0), stop=(cc == 7))
                    ins.then_inc(PEs, 1)
                for dh in range(2):
                    e.wait_ge(DVs, kact_ct[7])
                    if t > 0:
                        e.wait_ge(DVs, final_ct[(t - 1, dh)] - 2)
                    elif ch > 0:
                        e.wait_ge(DVs, CCH[ch - 1]['final'][(3, dh)] - 2)
                    for oc in range(8):
                        ins = e.matmul(pvC[:, dh * 512:(dh + 1) * 512],
                                       s_kact[:, (b * 8 + oc) * 512 + t * 128:(b * 8 + oc) * 512 + (t + 1) * 128],
                                       s_wvT[:, oc * C + dh * 512:oc * C + (dh + 1) * 512],
                                       start=(oc == 0), stop=(oc == 7))
                    ins.then_inc(PEs, 1)
                for dh in range(2):
                    e.wait_ge(DVs, attnN_ct[7])
                    if t > 0:
                        e.wait_ge(DVs, final_ct[(t - 1, dh)] - 1)
                    elif ch > 0:
                        e.wait_ge(DVs, CCH[ch - 1]['final'][(3, dh)] - 1)
                    for g in range(8):
                        ins = e.matmul(pwC[:, dh * 512:(dh + 1) * 512],
                                       s_attc[:, (b * 8 + g) * 512 + t * 128:(b * 8 + g) * 512 + (t + 1) * 128],
                                       s_wo16[:, g * C + dh * 512:g * C + (dh + 1) * 512],
                                       start=(g == 0), stop=(g == 7))
                    ins.then_inc(PEs, 1)
        n['PE'] = pec

    store_state['SOa'] = 16 * ((4 * nC + 1) // 2)
    store_state['SOb'] = 16 * ((4 * nC) // 2)

    return _finish([psC], [C_ar], [cw_ar])


_CONSTS = None
_NC = None


def kernel(L0_attn_x, L0_attn_Wq, L0_attn_bq, L0_attn_Wk, L0_attn_bk,
           L0_attn_Wv, L0_attn_bv, L0_attn_Wo, L0_attn_bo,
           L0_ffn_W_r, L0_ffn_W_k, L0_ffn_W_v, L0_ffn_mu_r, L0_ffn_mu_k):
    global _CONSTS, _NC
    if _CONSTS is None:
        _CONSTS = _host_consts()
    cst = _CONSTS
    x = np.ascontiguousarray(np.asarray(L0_attn_x, dtype=np.float32))
    hperm = [8 * (g // 4) + (g % 4) + 4 * h for g in range(8) for h in range(2)]
    colperm = np.concatenate([np.arange(h * 64, (h + 1) * 64) for h in hperm])
    shared = dict(
        wq=np.ascontiguousarray(np.asarray(L0_attn_Wq, np.float32)[:, colperm]),
        wk=np.ascontiguousarray(L0_attn_Wk, np.float32),
        wv=np.ascontiguousarray(L0_attn_Wv, np.float32),
        wo=np.ascontiguousarray(np.asarray(L0_attn_Wo, np.float32)[colperm, :]),
        wrT=np.ascontiguousarray(np.asarray(L0_ffn_W_r, np.float32).T),
        wkT=np.ascontiguousarray(np.asarray(L0_ffn_W_k, np.float32).T),
        wvT=np.ascontiguousarray(np.asarray(L0_ffn_W_v, np.float32).T),
        bqT=np.ascontiguousarray(np.asarray(L0_attn_bq, np.float32)[colperm].reshape(8, 128).T),
        bkT=np.ascontiguousarray(np.asarray(L0_attn_bk, np.float32).reshape(2, 128).T),
        bv=np.ascontiguousarray(np.asarray(L0_attn_bv, np.float32).reshape(1, 256)),
        bo=np.ascontiguousarray(np.asarray(L0_attn_bo, np.float32).reshape(1, C)),
        muR=np.ascontiguousarray(np.asarray(L0_ffn_mu_r, np.float32).reshape(8, 128).T),
        muK=np.ascontiguousarray(np.asarray(L0_ffn_mu_k, np.float32).reshape(8, 128).T),
        ident=cst['ident'], tri=cst['tri'], onescol=cst['onescol'],
        pat2=cst['pat2'], pat2T=cst['pat2T'],
        ones1f=cst['ones1f'], epst=cst['epst'],
        zcol=cst['zcol'], zrow=cst['zrow'],
        cosrep=cst['cosrep'], sinrep=cst['sinrep'], permQ=cst['permQ'],
    )
    if _NC is None:
        _NC = build_program()
    in_maps = [dict(shared, x=np.ascontiguousarray(x[i])) for i in range(8)]
    ncore = int(os.environ.get("K_CORES", "8"))
    res = run_bass_kernel_spmd(_NC, in_maps[:ncore], core_ids=list(range(ncore)))
    outs = [np.asarray(r["out"], np.float32) for r in res.results]
    while len(outs) < 8:
        outs.append(np.zeros((S, C), np.float32))
    return np.stack(outs, axis=0)
